# version 02
# baseline (speedup 1.0000x reference)
import contextlib
import numpy as np
import ml_dtypes
import concourse.bass as bass
import concourse.mybir as mybir
from concourse.bass_utils import run_bass_kernel_spmd

F32 = mybir.dt.float32
BF16 = mybir.dt.bfloat16
I32 = mybir.dt.int32
AF = mybir.ActivationFunctionType
ALU = mybir.AluOpType
AX = mybir.AxisListType

D = 1024
H = 16
HN = 64
DH = 4096
NCH = 8
NHID = 32
EPS_LN = 1e-5
EPS_GN = 64e-5


class Prog:
    def __init__(self, nc, stack):
        self.nc = nc
        self.stack = stack
        self.sems = {}
        self.cnt = {}

    def sem(self, name):
        self.sems[name] = self.stack.enter_context(self.nc.semaphore(name))
        self.cnt[name] = 0

    def inc(self, name, inst, k=1):
        inst.then_inc(self.sems[name], k)
        self.cnt[name] += k

    def wait(self, eng, name, val=None):
        v = self.cnt[name] if val is None else val
        if v > 0:
            eng.wait_ge(self.sems[name], v)

    def val(self, name):
        return self.cnt[name]

    def flush(self, eng, name, inst):
        self.inc(name, inst)
        eng.wait_ge(self.sems[name], self.cnt[name])


def build_program(T=4096, dump_x1=False, stub_ffn=False, dump_hsil=False, astage=99, dump_attn=False, fstage=99, qstage=99):
    nc = bass.Bass("TRN2", target_bir_lowering=False, detect_race_conditions=False)
    S = T // 256
    assert T % 256 == 0

    x_in = nc.dram_tensor("x", [T, D], F32, kind="ExternalInput").ap()
    wT = {m: nc.dram_tensor(f"w{m}T", [D, D], F32, kind="ExternalInput").ap() for m in "rkvgo"}
    w1T = nc.dram_tensor("w1T", [D, DH], F32, kind="ExternalInput").ap()
    w2T = nc.dram_tensor("w2T", [DH, D], F32, kind="ExternalInput").ap()
    maa_in = nc.dram_tensor("maa", [128, 32], F32, kind="ExternalInput").ap()
    lnw_in = nc.dram_tensor("lnwbc", [128, D], F32, kind="ExternalInput").ap()
    decay_b = nc.dram_tensor("decay_b", [128, H], F32, kind="ExternalInput").ap()
    faaaa_b = nc.dram_tensor("faaaa_b", [128, H], F32, kind="ExternalInput").ap()
    decay_s = nc.dram_tensor("decay_s", [128, 16], F32, kind="ExternalInput").ap()
    lng_bc = nc.dram_tensor("lng_bc", [128, D], F32, kind="ExternalInput").ap()
    lnb_bc = nc.dram_tensor("lnb_bc", [128, D], F32, kind="ExternalInput").ap()
    convw_in = nc.dram_tensor("convw", [128, 3 * NHID], F32, kind="ExternalInput").ap()
    convb_in = nc.dram_tensor("convb", [128, NHID], F32, kind="ExternalInput").ap()

    out = nc.dram_tensor("out", [T, D], F32, kind="ExternalOutput").ap()
    x1_scr = nc.dram_tensor("x1_scr", [T, D], F32,
                            kind="ExternalOutput" if dump_x1 else "Internal").ap()
    dbg_h = nc.dram_tensor("dbg_h", [128, 32 * 260], F32,
                           kind="ExternalOutput" if dump_hsil else "Internal").ap()
    dbg_p = nc.dram_tensor("dbg_p", [128, 1024], F32,
                           kind="ExternalOutput" if dump_hsil else "Internal").ap()
    dbg_ab = nc.dram_tensor("dbg_ab", [128, 10240], BF16,
                            kind="ExternalOutput" if dump_attn else "Internal").ap()
    dbg_af = nc.dram_tensor("dbg_af", [128, 2048], F32,
                            kind="ExternalOutput" if dump_attn else "Internal").ap()
    w1q_scr = nc.dram_tensor("w1q_scr", [D, DH], BF16, kind="Internal").ap()
    w2q_scr = nc.dram_tensor("w2q_scr", [DH, D], BF16, kind="Internal").ap()

    eyeb_h = nc.inline_tensor(np.eye(128).astype(ml_dtypes.bfloat16), "eyeb_h").ap()
    eyef_h = nc.inline_tensor(np.eye(128, dtype=np.float32), "eyef_h").ap()
    iota_r = nc.inline_tensor(np.arange(128, dtype=np.float32).reshape(128, 1), "iota_r").ap()
    iota_k = nc.inline_tensor((-(np.arange(128, dtype=np.float32) + 1.0)).reshape(128, 1),
                              "iota_k").ap()
    jj, ii = np.meshgrid(np.arange(128), np.arange(128), indexing="ij")
    u01_h = nc.inline_tensor((jj < ii).astype(np.float32), "u01_h").ap()
    ones1_h = nc.inline_tensor(np.ones((1, 128), dtype=np.float32), "ones1_h").ap()

    stack = contextlib.ExitStack()
    p = Prog(nc, stack)

    with stack, nc.Block() as blk:
        for sn in ["dC", "dX", "dW", "dSx0", "dSx1", "dQ0", "dQ1", "dF",
                   "dO0", "dO1", "T", "V", "Sc", "G", "bar"]:
            p.sem(sn)

        def body(engname, fn):
            getattr(blk, engname)(fn)

        def alloc(st, name, shape, dtype=F32):
            return st.enter_context(nc.sbuf_tensor(name, shape, dtype))

        def palloc(st, name, shape, dtype=F32):
            return st.enter_context(nc.psum_tensor(name, shape, dtype))

        per = contextlib.ExitStack(); stack.enter_context(per)
        maa = alloc(per, "maa_s", [128, 32])
        eyeb = alloc(per, "eyeb", [128, 128], BF16)
        eyef = alloc(per, "eyef", [128, 128])
        ones1 = alloc(per, "ones1", [1, 128])
        stats = alloc(per, "stats", [128, 64])
        gstats = alloc(per, "gstats", [128, 48])
        smalls = alloc(per, "smalls", [128, 64])
        onecol = alloc(per, "onecol", [128, 8])
        convw_s = alloc(per, "convw_s", [128, 3 * NHID])
        convb_s = alloc(per, "convb_s", [128, NHID])

        att = contextlib.ExitStack(); stack.enter_context(att)
        WTs = {m: alloc(att, f"WT{m}", [128, NCH * D], BF16) for m in "rkvgo"}
        xbuf = [alloc(att, f"xbuf{i}", [128, 2048]) for i in range(2)]
        ln0tm = alloc(att, "ln0tm", [128, 2048], BF16)
        ln0T = alloc(att, "ln0T", [128, NCH * 257], BF16)
        xxb = alloc(att, "xxb", [128, 2048], BF16)
        mixb = [alloc(att, f"mixb{i}", [128, 2048], BF16) for i in range(2)]
        rtm = alloc(att, "rtm", [128, 2048], BF16)
        ktok = alloc(att, "ktok", [128, 2048], BF16)
        vtm = alloc(att, "vtm", [128, 2048], BF16)
        gtm = alloc(att, "gtm", [128, 2048], BF16)
        rfm = alloc(att, "rfm64", [64, 4096], BF16)
        kfm = alloc(att, "kfm64", [64, 4096], BF16)
        deckR = alloc(att, "deckR", [128, D], BF16)
        deckK = alloc(att, "deckK", [128, D], BF16)
        wmask = alloc(att, "wmask", [128, H * 128], BF16)
        aring = alloc(att, "aring", [128, 4 * 128], BF16)
        yfm = alloc(att, "yfm", [128, D])
        ytm = alloc(att, "ytm", [128, D])
        ytotm = alloc(att, "ytotm", [128, D], BF16)
        ytotT = alloc(att, "ytotT", [128, 2048], BF16)
        aot = alloc(att, "aot", [128, D])
        sqdmp = alloc(att, "sqdmp", [128, 1024], BF16)
        sS = alloc(att, "sS", [64, 1024])
        sSb = alloc(att, "sSb", [64, 1024], BF16)
        wQt = alloc(att, "wQt", [128, 16])
        lngb = alloc(att, "lngb", [128, D])
        lnbb = alloc(att, "lnbb", [128, D])
        wstg = [alloc(att, f"wstg{i}", [128, 1024]) for i in range(2)]
        lnwbc = aot
        wqbf = [alloc(att, f"wqbf{i}", [128, 1024], BF16) for i in range(2)]

        patt = contextlib.ExitStack(); stack.enter_context(patt)
        PB = [palloc(patt, f"PB{i}", [128, 512]) for i in range(4)]
        PBbf = palloc(patt, "PBbf", [128, 1024], BF16)
        PY = palloc(patt, "PY", [128, 512])
        PA = palloc(patt, "PA", [128, 512])
        PT = palloc(patt, "PT", [64, 512])

        def pro_sync(sync):
            loads = [(maa[:, :], maa_in), (eyeb[:, :], eyeb_h), (eyef[:, :], eyef_h),
                     (ones1[:, :], ones1_h), (lnwbc[:, :], lnw_in),
                     (stats[:, 0:H], decay_b), (smalls[:, 16:32], faaaa_b),
                     (gstats[:, 0:16], decay_s),
                     (lngb[:, :], lng_bc), (lnbb[:, :], lnb_bc),
                     (convw_s[:, :], convw_in), (convb_s[:, :], convb_in),
                     (wstg[0][:, 0:128], u01_h),
                     (smalls[:, 40:41], iota_r), (smalls[:, 41:42], iota_k)]
            for d_, s_ in loads:
                p.inc("dC", sync.dma_start(d_, s_), 16)
        body("sync", pro_sync)
        pro_dY = p.val("dC")

        def pro_scal(sc):
            p.wait(sc, "dC", pro_dY)
            p.flush(sc, "Sc", sc.activation(lnwbc[:, :], lnwbc[:, :], AF.Exp))
            p.flush(sc, "Sc", sc.activation(lnwbc[:, :], lnwbc[:, :], AF.Identity, scale=-1.0))
            p.flush(sc, "Sc", sc.activation(stats[:, 0:H], stats[:, 0:H], AF.Exp))
            p.flush(sc, "Sc", sc.activation(stats[:, 0:H], stats[:, 0:H], AF.Identity,
                                            scale=-1.0))
            p.flush(sc, "Sc", sc.activation(stats[:, 0:H], stats[:, 0:H], AF.Exp))
            p.flush(sc, "Sc", sc.activation(gstats[:, 0:16], gstats[:, 0:16], AF.Exp))
            p.flush(sc, "Sc", sc.activation(gstats[:, 0:16], gstats[:, 0:16], AF.Identity,
                                            scale=-128.0))
            p.inc("Sc", sc.activation(wQt[:, :], gstats[:, 0:16], AF.Exp))
        body("scalar", pro_scal)
        pro_sc = p.val("Sc")

        def pro_vec(v):
            p.wait(v, "Sc", pro_sc)
            p.flush(v, "V", v.tensor_mul(smalls[:, 16:32], smalls[:, 16:32],
                                         stats[:, 0:H]))
            for h in range(H):
                v.scalar_tensor_tensor(wmask[:, h * 128:(h + 1) * 128], eyef[:, :],
                                       smalls[:, 16 + h:17 + h], wstg[0][:, 0:128],
                                       ALU.mult, ALU.add)
            v.memset(sS[0:64, :], 0.0)
            v.memset(sSb[0:64, :], 0.0)
            v.memset(onecol[:, 2:3], 1.0)
            ln0T_c = ln0T[:, :].rearrange("q (c w) -> q c w", w=257)
            p.inc("V", v.memset(ln0T_c[:, :, 0:1], 0.0))
        body("vector", pro_vec)
        pro_v = p.val("V")

        def magic_v(v):
            p.flush(v, "V", v.memset(smalls[:, 43:44], 1.0))
            p.flush(v, "V", v.tensor_scalar(smalls[:, 44:45], smalls[:, 43:44], 2.0 ** -29,
                                            None, ALU.add))
            p.flush(v, "V", v.tensor_scalar(smalls[:, 44:45], smalls[:, 44:45], 1.0, None,
                                            ALU.subtract))
            p.flush(v, "V", v.tensor_scalar(smalls[:, 44:45], smalls[:, 44:45], 2.0 ** 29,
                                            None, ALU.mult))
            v.tensor_scalar(smalls[:, 42:43], smalls[:, 44:45],
                            (1.5 * 2.0 ** 52) - (1.5 * 2.0 ** 23), 1.5 * 2.0 ** 23,
                            ALU.mult, ALU.add)
            p.inc("V", v.memset(smalls[:, 45:46], 0.0))
        body("vector", magic_v)

        def deck_scal(sc):
            p.wait(sc, "V", pro_v)
            p.wait(sc, "Sc")
            sc.activation(deckK[:, :], lnwbc[:, :], AF.Exp, scale=smalls[:, 41:42])
            p.inc("Sc", sc.activation(deckR[:, :], lnwbc[:, :], AF.Exp, scale=smalls[:, 40:41]))
        body("scalar", deck_scal)

        NW = 5 * NCH
        wl_casts = []
        for i in range(NW):
            m = "rkvgo"[i // NCH]; k = i % NCH

            def wl_s(sync, m=m, k=k, i=i):
                p.wait(sync, "V", wl_casts[i - 2] if i >= 2 else p.val("V"))
                p.inc("dW", sync.dma_start(wstg[i % 2][:, :],
                                           wT[m][k * 128:(k + 1) * 128, :]), 16)
            body("sync", wl_s)

            def wl_v(v, m=m, k=k, i=i):
                p.wait(v, "dW")
                p.inc("V", v.tensor_copy(WTs[m][:, k * D:(k + 1) * D], wstg[i % 2][:, :]))
            body("vector", wl_v)
            wl_casts.append(p.val("V"))

        SUMC, SQC, MUC, E2C, M2C, VARC, RSTDC, NBC = 0, 2, 4, 6, 8, 10, 12, 14
        x1_store_cnt = [0] * S
        pbbf_free = p.val("V")
        pb_free = [p.val("Sc") if False else 0] * 8
        last_amask = 0
        prev_ao_sc = 0

        for s in range(S):
            sb = s % 2

            def ld_x(sync, s=s, sb=sb):
                if s >= 2:
                    p.wait(sync, f"dSx{sb}", x1_store_cnt[s - 2])
                for g in range(2):
                    p.inc("dX", sync.dma_start(
                        xbuf[sb][:, g * 1024:(g + 1) * 1024],
                        x_in[s * 256 + g * 128: s * 256 + (g + 1) * 128, :]), 16)
            body("sync", ld_x)
            xload = p.val("dX")

            if astage >= 1:
                def ln0_v1(v, sb=sb, xload=xload):
                    p.wait(v, "dX", xload)
                    for g in range(2):
                        p.inc("V", v.reduce_sum(stats[:, SUMC + g:SUMC + g + 1].unsqueeze(2),
                                                xbuf[sb][:, g * 1024:(g + 1) * 1024].unsqueeze(1),
                                                AX.X))
                body("vector", ln0_v1)
    
                def ln0_s1(sc, sb=sb, xload=xload):
                    p.wait(sc, "dX", xload)
                    p.wait(sc, "T")
                    for g in range(2):
                        p.inc("Sc", sc.activation(sqdmp[:, :], xbuf[sb][:, g * 1024:(g + 1) * 1024],
                                                  AF.Square,
                                                  accum_out=stats[:, SQC + g:SQC + g + 1]))
                body("scalar", ln0_s1)
                sq_c = p.val("Sc")
    
                def ln0_v2(v, sq_c=sq_c):
                    p.wait(v, "Sc", sq_c)
                    p.wait(v, "V")
                    for g in range(2):
                        v.tensor_scalar(stats[:, MUC + g:MUC + g + 1],
                                        stats[:, SUMC + g:SUMC + g + 1], 1.0 / D, None, ALU.mult)
                        p.flush(v, "V", v.tensor_scalar(stats[:, E2C + g:E2C + g + 1],
                                        stats[:, SQC + g:SQC + g + 1], 1.0 / D, EPS_LN,
                                        ALU.mult, ALU.add))
                        p.flush(v, "V", v.tensor_mul(stats[:, M2C + g:M2C + g + 1],
                                     stats[:, MUC + g:MUC + g + 1],
                                     stats[:, MUC + g:MUC + g + 1]))
                        p.flush(v, "V", v.tensor_sub(stats[:, VARC + g:VARC + g + 1],
                                     stats[:, E2C + g:E2C + g + 1],
                                     stats[:, M2C + g:M2C + g + 1]))
                        p.inc("V", v.reciprocal(stats[:, VARC + g:VARC + g + 1],
                                                stats[:, VARC + g:VARC + g + 1]))
                body("vector", ln0_v2)
                lv2 = p.val("V")
    
                def ln0_s2(sc, lv2=lv2):
                    p.wait(sc, "V", lv2)
                    for g in range(2):
                        inst = sc.activation(stats[:, RSTDC + g:RSTDC + g + 1],
                                             stats[:, VARC + g:VARC + g + 1], AF.Sqrt)
                    p.inc("Sc", inst)
                body("scalar", ln0_s2)
                ls2 = p.val("Sc")
    
                def ln0_v3(v, ls2=ls2):
                    p.wait(v, "Sc", ls2)
                    p.wait(v, "V")
                    for g in range(2):
                        p.inc("V", v.scalar_tensor_tensor(
                            stats[:, NBC + g:NBC + g + 1], stats[:, MUC + g:MUC + g + 1],
                            -1.0, stats[:, RSTDC + g:RSTDC + g + 1], ALU.mult, ALU.mult))
                body("vector", ln0_v3)
                lv3 = p.val("V")
    
                def ln0_s3(sc, sb=sb, lv3=lv3):
                    p.wait(sc, "V", lv3)
                    p.wait(sc, "Sc")
                    for g in range(2):
                        p.inc("Sc", sc.activation(ln0tm[:, g * 1024:(g + 1) * 1024],
                                                  xbuf[sb][:, g * 1024:(g + 1) * 1024], AF.Identity,
                                                  bias=stats[:, NBC + g:NBC + g + 1],
                                                  scale=stats[:, RSTDC + g:RSTDC + g + 1]))
                body("scalar", ln0_s3)
                ln0_c = p.val("Sc")
    
                def bnd_v(v, s=s):
                    if s > 0:
                        ln0T_c = ln0T[:, :].rearrange("q (c w) -> q c w", w=257)
                        p.inc("V", v.tensor_copy(ln0T_c[:, :, 0:1], ln0T_c[:, :, 256:257]))
                body("vector", bnd_v)
                bnd_c = p.val("V")
    
            if astage >= 2:
                for g in range(2):
                    def trl(t, g=g, ln0_c=ln0_c, bnd_c=bnd_c):
                        p.wait(t, "Sc", ln0_c)
                        p.wait(t, "V", max(bnd_c, pbbf_free))
                        for c in range(NCH):
                            inst = t.transpose(PBbf[:, c * 128:(c + 1) * 128],
                                               ln0tm[:, g * 1024 + c * 128: g * 1024 + (c + 1) * 128],
                                               eyeb[:, :])
                        p.inc("T", inst)
                    body("tensor", trl)
                    trl_c = p.val("T")
    
                    def evl(v, g=g, trl_c=trl_c):
                        p.wait(v, "T", trl_c)
                        ln0T_c = ln0T[:, :].rearrange("q (c w) -> q c w", w=257)
                        src = PBbf[:, :].rearrange("q (c w) -> q c w", w=128)
                        p.inc("V", v.tensor_copy(ln0T_c[:, :, 1 + g * 128: 1 + (g + 1) * 128], src))
                    body("vector", evl)
                    pbbf_free = p.val("V")
    
            if astage >= 3:
                if dump_attn and s == 0:
                    def d1(sync):
                        p.wait(sync, "V")
                        p.wait(sync, "Sc")
                        p.inc("dC", sync.dma_start(dbg_ab[:, 8192:10240], ln0tm[:, :]), 16)
                        p.inc("dC", sync.dma_start(dbg_af[:, 0:64], stats[:, :]), 16)
                        p.inc("dC", sync.dma_start(dbg_ab[:, 0:2056], ln0T[:, :]), 16)
                    body("sync", d1)

                def xx_v(v):
                    ln0T_c = ln0T[:, :].rearrange("q (c w) -> q c w", w=257)
                    xx_c = xxb[:, :].rearrange("q (c w) -> q c w", w=256)
                    p.inc("V", v.tensor_tensor(xx_c, ln0T_c[:, :, 0:256], ln0T_c[:, :, 1:257],
                                               ALU.subtract))
                body("vector", xx_v)
                xx_c_ = p.val("V")
    
            if astage >= 4:
                mix_c = {}
                proj_c = {}
                evac_c = {}
                for mi, m in enumerate("rkvg"):
                    mb = mixb[mi % 2]
    
                    def mk_mix(v, m=m, mi=mi, mb=mb, xx_c_=xx_c_):
                        p.wait(v, "V", xx_c_)
                        if mi >= 2:
                            p.wait(v, "T", proj_c["rkvg"[mi - 2]])
                        ln0T_c = ln0T[:, :].rearrange("q (c w) -> q c w", w=257)
                        for c in range(NCH):
                            inst = v.scalar_tensor_tensor(
                                mb[:, c * 256:(c + 1) * 256], xxb[:, c * 256:(c + 1) * 256],
                                maa[:, mi * 8 + c: mi * 8 + c + 1], ln0T_c[:, c, 1:257],
                                ALU.mult, ALU.add)
                        p.inc("V", inst)
                    body("vector", mk_mix)
                    mix_c[m] = p.val("V")
    
                    for g in range(2):
                        step = mi * 2 + g
                        bank0 = PB[(step % 2) * 2]
                        bank1 = PB[(step % 2) * 2 + 1]
    
                        def mk_proj(t, m=m, mi=mi, g=g, mb=mb, step=step, bank0=bank0, bank1=bank1,
                                    prev_ao_sc=prev_ao_sc):
                            p.wait(t, "V", mix_c[m])
                            if step >= 2:
                                ec = evac_c[step - 2]
                                if ec[0] == "V":
                                    p.wait(t, "V", ec[1])
                                else:
                                    p.wait(t, "Sc", ec[1])
                            elif prev_ao_sc > 0:
                                p.wait(t, "Sc", prev_ao_sc)
                            for k in range(NCH):
                                lhs = mb[:, k * 256 + g * 128: k * 256 + (g + 1) * 128]
                                t.matmul(bank0[:, :], lhs, WTs[m][:, k * D: k * D + 512],
                                         start=(k == 0), stop=(k == NCH - 1))
                                inst = t.matmul(bank1[:, :], lhs, WTs[m][:, k * D + 512:(k + 1) * D],
                                                start=(k == 0), stop=(k == NCH - 1))
                            p.inc("T", inst)
                        body("tensor", mk_proj)
                        pj_c = p.val("T")
    
                        if m == "g":
                            def ev_g(sc, g=g, bank0=bank0, bank1=bank1, pj_c=pj_c):
                                p.wait(sc, "T", pj_c)
                                sc.activation(gtm[:, g * 1024: g * 1024 + 512], bank0[:, :], AF.Silu)
                                p.inc("Sc", sc.activation(gtm[:, g * 1024 + 512:(g + 1) * 1024],
                                                          bank1[:, :], AF.Silu))
                            body("scalar", ev_g)
                            evac_c[step] = ("Sc", p.val("Sc"))
                        else:
                            def ev_p(v, m=m, g=g, bank0=bank0, bank1=bank1, pj_c=pj_c):
                                p.wait(v, "T", pj_c)
                                dst = {"r": rtm, "k": ktok, "v": vtm}[m]
                                if m == "r":
                                    v.tensor_mul(dst[:, g * 1024:g * 1024 + 512], bank0[:, :],
                                                 deckR[:, 0:512])
                                    inst = v.tensor_mul(dst[:, g * 1024 + 512:(g + 1) * 1024],
                                                        bank1[:, :], deckR[:, 512:1024])
                                elif m == "k":
                                    v.tensor_mul(dst[:, g * 1024:g * 1024 + 512], bank0[:, :],
                                                 deckK[:, 0:512])
                                    inst = v.tensor_mul(dst[:, g * 1024 + 512:(g + 1) * 1024],
                                                        bank1[:, :], deckK[:, 512:1024])
                                else:
                                    v.tensor_copy(dst[:, g * 1024:g * 1024 + 512], bank0[:, :])
                                    inst = v.tensor_copy(dst[:, g * 1024 + 512:(g + 1) * 1024],
                                                         bank1[:, :])
                                p.inc("V", inst)
                            body("vector", ev_p)
                            evac_c[step] = ("V", p.val("V"))
                    proj_c[m] = p.val("T")
                rk_ev = evac_c[3][1]
                v_ev = evac_c[5][1]
                g_ev = evac_c[7][1]
    
            if dump_attn and s == 0:
                def d2(sync):
                    p.wait(sync, "V")
                    p.inc("dC", sync.dma_start(dbg_ab[:, 2560:4608], rtm[:, :]), 16)
                body("sync", d2)

            if astage >= 5:
                for src_t, dst_t in [(rtm, rfm), (ktok, kfm)]:
                    for g in range(2):
                        for hh in range(2):
                            def trk(t, src_t=src_t, g=g, hh=hh, rk_ev=rk_ev, pf=pbbf_free):
                                p.wait(t, "V", max(rk_ev, pf))
                                for j in range(8):
                                    h = hh * 8 + j
                                    inst = t.transpose(
                                        PBbf[0:64, j * 128:(j + 1) * 128],
                                        src_t[:, g * 1024 + h * 64: g * 1024 + (h + 1) * 64],
                                        eyeb[:, :])
                                p.inc("T", inst)
                            body("tensor", trk)
                            trk_c = p.val("T")

                            def evk(v, dst_t=dst_t, g=g, hh=hh, trk_c=trk_c):
                                p.wait(v, "T", trk_c)
                                dfm = dst_t[0:64, :].rearrange("q (h w) -> q h w", w=256)
                                srcp = PBbf[0:64, :].rearrange("q (j w) -> q j w", w=128)
                                p.inc("V", v.tensor_copy(
                                    dfm[:, hh * 8:(hh + 1) * 8, g * 128:(g + 1) * 128], srcp))
                            body("vector", evk)
                            pbbf_free = p.val("V")

            evrk = p.val("V")
    
            if astage >= 6:
                if dump_attn and s == 0:
                    def d3(sync):
                        p.wait(sync, "V")
                        p.inc("dC", sync.dma_start(dbg_ab[0:64, 4608:8192],
                                                   rfm[0:64, 0:3584]), 16)
                    body("sync", d3)

                for q in range(2):
                    amask_c = {}
                    ymm_c = {}
                    supd_half = {}
                    for hg in range(4):
                        def sc_a(t, q=q, hg=hg, evrk=evrk, v_ev=v_ev, la=last_amask):
                            p.wait(t, "V", evrk)
                            p.wait(t, "V", v_ev)
                            p.wait(t, "V", amask_c[hg - 1] if hg > 0 else la)
                            for j in range(4):
                                h = hg * 4 + j
                                cc = h * 256 + q * 128
                                inst = t.matmul(PA[:, j * 128:(j + 1) * 128],
                                                kfm[0:64, cc:cc + 128],
                                                rfm[0:64, cc:cc + 128],
                                                start=True, stop=True)
                            p.inc("T", inst)
                        body("tensor", sc_a)
                        a_c = p.val("T")

                        def sc_m(v, hg=hg, a_c=a_c):
                            p.wait(v, "T", a_c)
                            for j in range(4):
                                h = hg * 4 + j
                                inst = v.tensor_mul(aring[:, j * 128:(j + 1) * 128],
                                                    PA[:, j * 128:(j + 1) * 128],
                                                    wmask[:, h * 128:(h + 1) * 128])
                            p.inc("V", inst)
                        body("vector", sc_m)
                        amask_c[hg] = p.val("V")
                        last_amask = amask_c[hg]

                        def sc_y(t, q=q, hg=hg, g_ev=g_ev):
                            p.wait(t, "V", amask_c[hg])
                            if hg == 0:
                                p.wait(t, "Sc", g_ev)
                            for j in range(4):
                                h = hg * 4 + j
                                pg = (h % 2) * 64
                                cg = (h // 2) % 4
                                cc = h * 256 + q * 128
                                t.matmul(PY[pg:pg + 64, cg * 128:(cg + 1) * 128],
                                         sSb[0:64, h * 64:(h + 1) * 64],
                                         rfm[0:64, cc:cc + 128],
                                         start=True, stop=False, tile_position=(0, pg))
                                t.matmul(PY[pg:pg + 64, cg * 128:(cg + 1) * 128],
                                         vtm[:, q * 1024 + h * 64: q * 1024 + (h + 1) * 64],
                                         aring[:, j * 128:(j + 1) * 128],
                                         start=False, stop=True, tile_position=(0, pg))
                                inst = t.matmul(PT[0:64, (h % 8) * 64:(h % 8 + 1) * 64],
                                                ktok[:, q * 1024 + h * 64: q * 1024 + (h + 1) * 64],
                                                vtm[:, q * 1024 + h * 64: q * 1024 + (h + 1) * 64],
                                                start=True, stop=True)
                            p.inc("T", inst)
                        body("tensor", sc_y)
                        ymm_c[hg] = p.val("T")

                        if hg % 2 == 1:
                            half = hg // 2

                            def ev_y(v, hg=hg):
                                p.wait(v, "T", ymm_c[hg])
                                half_ = hg // 2
                                p.inc("V", v.tensor_copy(yfm[:, half_ * 512:(half_ + 1) * 512],
                                                         PY[:, :]))
                            body("vector", ev_y)

                            def s_upd(v, q=q, half=half, hg=hg):
                                p.wait(v, "T", ymm_c[hg])
                                sl = slice(half * 512, (half + 1) * 512)
                                v.tensor_add(sS[0:64, sl], sS[0:64, sl], PT[0:64, :])
                                wq_b = wQt[0:64, half * 8:(half + 1) * 8].unsqueeze(2)\
                                    .broadcast_to([64, 8, 64])
                                v.tensor_tensor(sS[0:64, sl].rearrange("q (c w) -> q c w", w=64),
                                                sS[0:64, sl].rearrange("q (c w) -> q c w", w=64),
                                                wq_b, ALU.mult)
                                p.inc("V", v.tensor_copy(sSb[0:64, sl], sS[0:64, sl]))
                            body("vector", s_upd)
                            supd_half[half] = p.val("V")
                    yfm_c = p.val("V")

                    def tr_y(t, q=q, yfm_c=yfm_c, g_ev=g_ev):
                        p.wait(t, "V", yfm_c)
                        p.wait(t, "Sc", g_ev)
                        for cg in range(8):
                            bank = PB[2] if cg < 4 else PB[3]
                            inst = t.transpose(bank[:, (cg % 4) * 128:(cg % 4 + 1) * 128],
                                               yfm[:, cg * 128:(cg + 1) * 128], eyef[:, :])
                            if cg in (3, 7):
                                p.inc("T", inst)
                    body("tensor", tr_y)
                    try_c = p.val("T")

                    def ev_ytm(v, try_c=try_c):
                        p.wait(v, "T", try_c - 1)
                        v.tensor_copy(ytm[:, 0:512], PB[2][:, :])
                        p.wait(v, "T", try_c)
                        p.inc("V", v.tensor_copy(ytm[:, 512:1024], PB[3][:, :]))
                    body("vector", ev_ytm)
                    evy_c = p.val("V")

                    if dump_attn and s == 0 and q == 0:
                        def d4(sync, evy_c=evy_c):
                            p.wait(sync, "V", evy_c)
                            p.inc("dC", sync.dma_start(dbg_af[:, 0:1024], ytm[:, :]), 16)
                            p.inc("dC", sync.dma_start(dbg_af[:, 1024:2048], yfm[:, :]), 16)
                        body("sync", d4)

                    def gn_s(sc, evy_c=evy_c):
                        p.wait(sc, "V", evy_c)
                        p.inc("Sc", sc.activation(sqdmp[:, :], ytm[:, :], AF.Square))
                    body("scalar", gn_s)
                    gsq_c = p.val("Sc")

                    def gn_v(v, gsq_c=gsq_c):
                        p.wait(v, "Sc", gsq_c)
                        yv = ytm[:, :].rearrange("q (h w) -> q h w", w=HN)
                        sq = sqdmp[:, :].rearrange("q (h w) -> q h w", w=HN)
                        v.reduce_sum(gstats[:, 0:16].unsqueeze(2), yv, AX.X)
                        p.flush(v, "V", v.reduce_sum(gstats[:, 16:32].unsqueeze(2), sq, AX.X))
                        v.tensor_scalar(gstats[:, 0:16], gstats[:, 0:16], 1.0 / HN, None, ALU.mult)
                        p.flush(v, "V", v.tensor_scalar(gstats[:, 16:32], gstats[:, 16:32],
                                        1.0 / HN, EPS_GN, ALU.mult, ALU.add))
                        p.flush(v, "V", v.tensor_mul(gstats[:, 32:48], gstats[:, 0:16],
                                                     gstats[:, 0:16]))
                        p.flush(v, "V", v.tensor_sub(gstats[:, 16:32], gstats[:, 16:32],
                                                     gstats[:, 32:48]))
                        p.inc("V", v.reciprocal(gstats[:, 16:32], gstats[:, 16:32]))
                    body("vector", gn_v)
                    gnv_c = p.val("V")

                    def gn_s2(sc, gnv_c=gnv_c):
                        p.wait(sc, "V", gnv_c)
                        p.inc("Sc", sc.activation(gstats[:, 16:32], gstats[:, 16:32], AF.Sqrt))
                    body("scalar", gn_s2)
                    gns2_c = p.val("Sc")

                    def gn_ap(v, q=q, gns2_c=gns2_c):
                        p.wait(v, "Sc", gns2_c)
                        p.wait(v, "V")
                        if dump_attn and s == 0 and q == 0:
                            p.wait(v, "dC")
                        yv = ytm[:, :].rearrange("q (h w) -> q h w", w=HN)
                        mu_b = gstats[:, 0:16].unsqueeze(2).broadcast_to([128, 16, HN])
                        rs_b = gstats[:, 16:32].unsqueeze(2).broadcast_to([128, 16, HN])
                        v.tensor_tensor(yv, yv, mu_b, ALU.subtract)
                        v.tensor_tensor(yv, yv, rs_b, ALU.mult)
                        v.tensor_mul(ytm[:, :], ytm[:, :], lngb[:, :])
                        v.tensor_add(ytm[:, :], ytm[:, :], lnbb[:, :])
                        p.inc("V", v.tensor_mul(ytotm[:, :], ytm[:, :],
                                                gtm[:, q * 1024:(q + 1) * 1024]))
                    body("vector", gn_ap)
                    gna_c = p.val("V")

                    def tr_yt(t, gna_c=gna_c, pf=pbbf_free):
                        p.wait(t, "V", max(gna_c, pf))
                        for c in range(NCH):
                            inst = t.transpose(PBbf[:, c * 128:(c + 1) * 128],
                                               ytotm[:, c * 128:(c + 1) * 128], eyeb[:, :])
                        p.inc("T", inst)
                    body("tensor", tr_yt)
                    tryt_c = p.val("T")

                    def ev_yt(v, q=q, tryt_c=tryt_c):
                        p.wait(v, "T", tryt_c)
                        ytT = ytotT[:, :].rearrange("q (c w) -> q c w", w=256)
                        srcp = PBbf[:, :].rearrange("q (c w) -> q c w", w=128)
                        p.inc("V", v.tensor_copy(ytT[:, :, q * 128:(q + 1) * 128], srcp))
                    body("vector", ev_yt)
                    pbbf_free = p.val("V")
                evyt = p.val("V")

                for q in range(2):
                    def oproj(t, q=q, evyt=evyt):
                        p.wait(t, "V", evyt)
                        if q == 0:
                            ec = evac_c[6]
                            p.wait(t, "Sc", ec[1])
                        else:
                            p.wait(t, "Sc", evac_c[7][1])
                        for k in range(NCH):
                            lhs = ytotT[:, k * 256 + q * 128: k * 256 + (q + 1) * 128]
                            t.matmul(PB[q * 2][:, :], lhs, WTs["o"][:, k * D:k * D + 512],
                                     start=(k == 0), stop=(k == NCH - 1))
                            inst = t.matmul(PB[q * 2 + 1][:, :], lhs,
                                            WTs["o"][:, k * D + 512:(k + 1) * D],
                                            start=(k == 0), stop=(k == NCH - 1))
                        p.inc("T", inst)
                    body("tensor", oproj)
                    op_c = p.val("T")
    
                    def ao_ev(sc, q=q, op_c=op_c):
                        p.wait(sc, "T", op_c)
                        sc.copy(aot[:, 0:512], PB[q * 2][:, :])
                        sc.copy(aot[:, 512:1024], PB[q * 2 + 1][:, :])
                        p.inc("Sc", sc.activation(sqdmp[:, :], aot[:, :], AF.Square,
                                                  accum_out=stats[:, SQC:SQC + 1]))
                    body("scalar", ao_ev)
                    aoe_c = p.val("Sc")
    
                    def ao_v(v, aoe_c=aoe_c):
                        p.wait(v, "Sc", aoe_c)
                        p.flush(v, "V", v.reduce_sum(stats[:, SUMC:SUMC + 1].unsqueeze(2),
                                     aot[:, :].unsqueeze(1), AX.X))
                        v.tensor_scalar(stats[:, MUC:MUC + 1], stats[:, SUMC:SUMC + 1], 1.0 / D,
                                        None, ALU.mult)
                        p.flush(v, "V", v.tensor_scalar(stats[:, E2C:E2C + 1],
                                        stats[:, SQC:SQC + 1], 1.0 / D,
                                        EPS_LN, ALU.mult, ALU.add))
                        p.flush(v, "V", v.tensor_mul(stats[:, M2C:M2C + 1], stats[:, MUC:MUC + 1],
                                     stats[:, MUC:MUC + 1]))
                        p.flush(v, "V", v.tensor_sub(stats[:, VARC:VARC + 1], stats[:, E2C:E2C + 1],
                                     stats[:, M2C:M2C + 1]))
                        p.inc("V", v.reciprocal(stats[:, VARC:VARC + 1], stats[:, VARC:VARC + 1]))
                    body("vector", ao_v)
                    aov_c = p.val("V")
    
                    def ao_s2(sc, aov_c=aov_c):
                        p.wait(sc, "V", aov_c)
                        p.inc("Sc", sc.activation(stats[:, RSTDC:RSTDC + 1], stats[:, VARC:VARC + 1],
                                                  AF.Sqrt))
                    body("scalar", ao_s2)
                    aos2_c = p.val("Sc")
    
                    def ao_v2(v, aos2_c=aos2_c):
                        p.wait(v, "Sc", aos2_c)
                        p.wait(v, "V")
                        p.inc("V", v.scalar_tensor_tensor(
                            stats[:, NBC:NBC + 1], stats[:, MUC:MUC + 1], -1.0,
                            stats[:, RSTDC:RSTDC + 1], ALU.mult, ALU.mult))
                    body("vector", ao_v2)
                    aov2_c = p.val("V")
    
                    def ao_s3(sc, q=q, sb=sb, aov2_c=aov2_c):
                        p.wait(sc, "V", aov2_c)
                        p.wait(sc, "Sc")
                        p.inc("Sc", sc.activation(xbuf[sb][:, q * 1024:(q + 1) * 1024],
                                                  xbuf[sb][:, q * 1024:(q + 1) * 1024], AF.Identity,
                                                  bias=stats[:, NBC:NBC + 1]))
                    body("scalar", ao_s3)
                    aos3_c = p.val("Sc")
    
                    def ao_v3(v, q=q, sb=sb, aos3_c=aos3_c):
                        p.wait(v, "Sc", aos3_c)
                        p.inc("V", v.scalar_tensor_tensor(
                            xbuf[sb][:, q * 1024:(q + 1) * 1024], aot[:, :],
                            stats[:, RSTDC:RSTDC + 1], xbuf[sb][:, q * 1024:(q + 1) * 1024],
                            ALU.mult, ALU.add))
                    body("vector", ao_v3)
                    x1_c = p.val("V")
    
                    def x1_st(sc, q=q, s=s, sb=sb, x1_c=x1_c):
                        p.wait(sc, "V", x1_c)
                        p.inc(f"dSx{sb}", sc.dma_start(
                            x1_scr[s * 256 + q * 128: s * 256 + (q + 1) * 128, :],
                            xbuf[sb][:, q * 1024:(q + 1) * 1024]), 16)
                    body("scalar", x1_st)
                prev_ao_sc = p.val("Sc")
                x1_store_cnt[s] = p.val(f"dSx{sb}")

        def quant_weight(wsrc, wqdst, nrow, ncolp, mcol):
            pieces = [(r, c) for r in range(nrow) for c in range(ncolp)]

            def q_init(v):
                p.inc("V", v.memset(stats[:, 16:17], 0.0))
            body("vector", q_init)

            abs_cnts = []
            for i, (r, c) in enumerate(pieces):
                def q_ld(sync, r=r, c=c, i=i):
                    p.wait(sync, "V", abs_cnts[i - 2] if i >= 2 else p.val("V"))
                    p.inc("dW", sync.dma_start(
                        wstg[i % 2][:, :],
                        wsrc[r * 128:(r + 1) * 128, c * 1024:(c + 1) * 1024]), 16)
                body("sync", q_ld)

                def q_abs(v, i=i):
                    p.wait(v, "dW")
                    p.flush(v, "V", v.reduce_sum(stats[:, 17:18].unsqueeze(2),
                                 wstg[i % 2][:, :].unsqueeze(1),
                                 AX.X, apply_absolute_value=True))
                    p.inc("V", v.tensor_add(stats[:, 16:17], stats[:, 16:17], stats[:, 17:18]))
                body("vector", q_abs)
                abs_cnts.append(p.val("V"))

            def q_sum(t):
                p.wait(t, "V")
                p.inc("T", t.matmul(PA[0:1, 256:257], stats[:, 16:17], onecol[:, 2:3],
                                    start=True, stop=True))
            body("tensor", q_sum)

            nelem = nrow * 128 * ncolp * 1024

            def q_m(v):
                p.wait(v, "T")
                p.inc("V", v.tensor_scalar(smalls[0:1, 62:63], PA[0:1, 256:257], 1.0 / nelem,
                                           1e-5, ALU.mult, ALU.max))
            body("vector", q_m)

            def q_bc(t):
                p.wait(t, "V")
                p.inc("T", t.matmul(PA[:, 0:1], ones1[:, :], smalls[0:1, 62:63],
                                    start=True, stop=True))
            body("tensor", q_bc)

            def q_f(v):
                p.wait(v, "T")
                v.tensor_copy(smalls[:, mcol:mcol + 1], PA[:, 0:1])
                p.inc("V", v.reciprocal(smalls[:, mcol + 1:mcol + 2], PA[:, 0:1]))
            body("vector", q_f)

            rnd_cnts = []
            st_cnts = {0: [], 1: []}
            for i, (r, c) in enumerate(pieces):
                def q_ld2(sync, r=r, c=c, i=i):
                    p.wait(sync, "V", rnd_cnts[i - 2] if i >= 2 else p.val("V"))
                    p.inc("dW", sync.dma_start(
                        wstg[i % 2][:, :],
                        wsrc[r * 128:(r + 1) * 128, c * 1024:(c + 1) * 1024]), 16)
                body("sync", q_ld2)

                def q_rnd(v, i=i):
                    p.wait(v, "dW")
                    p.wait(v, f"dQ{i % 2}")
                    v.tensor_scalar(wstg[i % 2][:, :], wstg[i % 2][:, :],
                                    smalls[:, mcol + 1:mcol + 2], 1.49, ALU.mult, ALU.min)
                    v.tensor_scalar(wstg[i % 2][:, :], wstg[i % 2][:, :], -1.49, None, ALU.max)
                    v.tensor_scalar(wstg[i % 2][:, :], wstg[i % 2][:, :],
                                    smalls[:, 42:43], None, ALU.add)
                    v.tensor_scalar(wstg[i % 2][:, :], wstg[i % 2][:, :],
                                    smalls[:, 42:43], None, ALU.subtract)
                    p.inc("V", v.tensor_copy(wqbf[i % 2][:, :], wstg[i % 2][:, :]))
                body("vector", q_rnd)
                rnd_cnts.append(p.val("V"))

                def q_st(sc, r=r, c=c, i=i):
                    p.wait(sc, "V", rnd_cnts[i])
                    p.inc(f"dQ{i % 2}", sc.dma_start(
                        wqdst[r * 128:(r + 1) * 128, c * 1024:(c + 1) * 1024],
                        wqbf[i % 2][:, :]), 16)
                body("scalar", q_st)

        quant_weight(w1T, w1q_scr, NCH, 4, 60)
        quant_weight(w2T, w2q_scr, NHID, 1, 58)

        def bar_sy(sync):
            for sn in ["dC", "dX", "dW"]:
                p.wait(sync, sn)
        body("sync", bar_sy)

        def bar_v(v):
            p.inc("bar", v.memset(onecol[:, 3:4], 0.0))
        body("vector", bar_v)

        def bar_s(sc):
            for sn in ["dSx0", "dSx1", "dQ0", "dQ1"]:
                p.wait(sc, sn)
            p.inc("bar", sc.copy(onecol[:, 4:5], onecol[:, 3:4]))
        body("scalar", bar_s)

        def bar_t(t):
            p.wait(t, "V")
            p.wait(t, "Sc")
            p.wait(t, "G")
            p.inc("bar", t.matmul(PT[0:1, 0:1], stats[0:1, 0:1], onecol[0:1, 2:3],
                                  start=True, stop=True))
        body("tensor", bar_t)

        def bar_g(g):
            p.inc("bar", g.memset(onecol[:, 5:6], 0.0))
        body("gpsimd", bar_g)
        barv = p.val("bar")

        att.close()
        patt.close()

        if stub_ffn:
            ffn = contextlib.ExitStack(); stack.enter_context(ffn)
            pffn = contextlib.ExitStack(); stack.enter_context(pffn)
            cpbuf = [alloc(ffn, f"cpbuf{i}", [128, 1024]) for i in range(2)]

            def fs_bar(sync):
                p.wait(sync, "bar", barv)
            body("sync", fs_bar)

            def fs_bar2(sc):
                p.wait(sc, "bar", barv)
            body("scalar", fs_bar2)

            st_cnt = []
            for i in range(T // 128):
                def fs_ld(sync, i=i):
                    if i >= 2:
                        p.wait(sync, f"dO{i % 2}", st_cnt[i - 2])
                    p.inc("dF", sync.dma_start(cpbuf[i % 2][:, :],
                                               x1_scr[i * 128:(i + 1) * 128, :]), 16)
                body("sync", fs_ld)

                def fs_st(sc, i=i):
                    p.wait(sc, "dF")
                    p.inc(f"dO{i % 2}", sc.dma_start(out[i * 128:(i + 1) * 128, :],
                                                     cpbuf[i % 2][:, :]), 16)
                body("scalar", fs_st)
                st_cnt.append(p.val(f"dO{i % 2}"))

            def fs_fin(sc):
                p.wait(sc, "dO0")
                p.wait(sc, "dO1")
            body("scalar", fs_fin)
        else:
            NT = T // 256
            ffn = contextlib.ExitStack(); stack.enter_context(ffn)
            pffn = contextlib.ExitStack(); stack.enter_context(pffn)
            w1qs = alloc(ffn, "w1qs", [128, NCH * DH], BF16)
            w2p = [alloc(ffn, f"w2p{i}", [128, 4 * D], BF16) for i in range(2)]
            x1a = [alloc(ffn, f"x1a{i}", [128, 2048]) for i in range(2)]
            x1s = [alloc(ffn, f"x1s{i}", [128, 1024]) for i in range(2)]
            outb = [alloc(ffn, f"outb{i}", [128, 1024]) for i in range(2)]
            xqb = alloc(ffn, "xqb", [128, 2048], BF16)
            xqf = alloc(ffn, "xqf", [128, 1024])
            xqT = alloc(ffn, "xqT", [128, 2048], BF16)
            hsil = alloc(ffn, "hsil", [128, 32 * 260])
            hbuf = [alloc(ffn, f"hbuf{i}", [128, 260]) for i in range(2)]
            cv1 = [alloc(ffn, f"cv1_{i}", [128, 260]) for i in range(2)]
            gtmp = [alloc(ffn, f"gtmp{i}", [128, 260]) for i in range(2)]
            mmap2 = alloc(ffn, "mmap2", [128, 520])
            carries = alloc(ffn, "carries", [128, 64])
            mmap = alloc(ffn, "mmap", [128, 260])
            hq = alloc(ffn, "hq", [128, 32 * 260], BF16)
            c1s = alloc(ffn, "c1s", [128, 256])
            rowsb = alloc(ffn, "rowsb", [1, 260])
            ao2 = alloc(ffn, "ao2", [128, 1024])
            fdump = alloc(ffn, "fdump", [128, 1024], BF16)

            Q0 = palloc(pffn, "Q0", [128, 512])
            Q1 = palloc(pffn, "Q1", [128, 512])
            C0 = palloc(pffn, "C0", [128, 1024])
            C1 = palloc(pffn, "C1", [128, 1024])
            Cbf = palloc(pffn, "Cbf", [128, 1024], BF16)
            QB = palloc(pffn, "QB", [128, 512])

            def tile_geom(t):
                c_lo = max(256 * t - 1, 0)
                c_hi = 256 * t + 255 if t < NT - 1 else T
                Wt = c_hi - c_lo
                if t == 0:
                    groups = [(0, 128), (128, 127)]
                elif t < NT - 1:
                    groups = [(0, 128), (128, 128)]
                else:
                    groups = [(0, 128), (128, 128), (256, 1)]
                return c_lo, c_hi, Wt, groups

            for en in ["sync", "vector", "scalar", "tensor", "gpsimd"]:
                def fb(e, en=en):
                    p.wait(e, "bar", barv)
                body(en, fb)

            def f_ld0(sync):
                for k in range(NCH):
                    p.inc("dF", sync.dma_start(w1qs[:, k * DH:(k + 1) * DH],
                                               w1q_scr[k * 128:(k + 1) * 128, :]), 16)
                for j in range(2):
                    for c in range(4):
                        p.inc("dF", sync.dma_start(
                            w2p[j][:, c * D:(c + 1) * D],
                            w2q_scr[(j * 4 + c) * 128:(j * 4 + c + 1) * 128, :]), 16)
            body("sync", f_ld0)

            def f_init(v):
                v.memset(carries[:, :], 0.0)
                hs_c = hsil[:, :].rearrange("q (c w) -> q c w", w=260)
                v.memset(hs_c[:, :, 258:260], 0.0)
                v.memset(hbuf[0][:, 258:260], 0.0)
                p.inc("V", v.memset(hbuf[1][:, 258:260], 0.0))
            body("vector", f_init)

            ln2_done = {}
            hq_done = [0] * NT
            mm2_done = [0] * (NT + 1)
            mm1_done = [0] * NT
            out_idx = [0]
            x1a_used = [0] * NT
            x1s_st = {}

            def emit_ln2(t):
                tb = t % 2

                def l_ld(sync, t=t, tb=tb):
                    if t >= 2:
                        p.wait(sync, "V", x1a_used[t - 2])
                    for g in range(2):
                        p.inc("dF", sync.dma_start(
                            x1a[tb][:, g * 1024:(g + 1) * 1024],
                            x1_scr[t * 256 + g * 128: t * 256 + (g + 1) * 128, :]), 16)
                body("sync", l_ld)

                for g in range(2):
                    def l_v1(v, t=t, tb=tb, g=g):
                        p.wait(v, "dF")
                        xg = x1a[tb][:, g * 1024:(g + 1) * 1024]
                        v.reduce_sum(stats[:, 0:1].unsqueeze(2), xg.unsqueeze(1), AX.X)
                        v.reduce_max(stats[:, 1:2].unsqueeze(2), xg.unsqueeze(1), AX.X)
                        p.inc("V", v.tensor_reduce(stats[:, 2:3].unsqueeze(2), xg.unsqueeze(1),
                                                   AX.X, ALU.min))
                    body("vector", l_v1)
                    lv1 = p.val("V")

                    def l_s1(sc, t=t, tb=tb, g=g, lv1=lv1):
                        p.wait(sc, "V", lv1)
                        p.inc("Sc", sc.activation(fdump[:, :], x1a[tb][:, g * 1024:(g + 1) * 1024],
                                                  AF.Square, accum_out=stats[:, 3:4]))
                    body("scalar", l_s1)
                    ls1 = p.val("Sc")

                    def l_v2(v, ls1=ls1):
                        p.wait(v, "Sc", ls1)
                        p.wait(v, "V")
                        v.tensor_scalar(stats[:, 4:5], stats[:, 0:1], 1.0 / D, None, ALU.mult)
                        p.flush(v, "V", v.tensor_scalar(stats[:, 5:6], stats[:, 3:4], 1.0 / D,
                                        EPS_LN, ALU.mult, ALU.add))
                        p.flush(v, "V", v.tensor_mul(stats[:, 6:7], stats[:, 4:5], stats[:, 4:5]))
                        p.flush(v, "V", v.tensor_sub(stats[:, 5:6], stats[:, 5:6], stats[:, 6:7]))
                        p.inc("V", v.reciprocal(stats[:, 5:6], stats[:, 5:6]))
                    body("vector", l_v2)
                    lv2 = p.val("V")

                    def l_s2(sc, lv2=lv2):
                        p.wait(sc, "V", lv2)
                        p.inc("Sc", sc.activation(stats[:, 5:6], stats[:, 5:6], AF.Sqrt))
                    body("scalar", l_s2)
                    ls2 = p.val("Sc")

                    def l_v3(v, g=g, ls2=ls2):
                        p.wait(v, "Sc", ls2)
                        p.wait(v, "V")
                        v.tensor_sub(stats[:, 7:8], stats[:, 1:2], stats[:, 4:5])
                        p.flush(v, "V", v.tensor_sub(stats[:, 8:9], stats[:, 4:5],
                                                     stats[:, 2:3]))
                        p.flush(v, "V", v.tensor_max(stats[:, 7:8], stats[:, 7:8],
                                                     stats[:, 8:9]))
                        p.flush(v, "V", v.scalar_tensor_tensor(stats[:, 7:8], stats[:, 7:8],
                                        1.0, stats[:, 5:6], ALU.mult, ALU.mult))
                        p.flush(v, "V", v.tensor_scalar(stats[:, 7:8], stats[:, 7:8], 1e-5,
                                        None, ALU.max))
                        p.flush(v, "V", v.reciprocal(stats[:, 9:10], stats[:, 7:8]))
                        p.flush(v, "V", v.tensor_scalar(stats[:, 9:10], stats[:, 9:10], 127.0,
                                        None, ALU.mult))
                        p.flush(v, "V", v.tensor_mul(stats[:, 10:11], stats[:, 5:6],
                                        stats[:, 9:10]))
                        v.scalar_tensor_tensor(stats[:, 11:12], stats[:, 4:5], -1.0,
                                               stats[:, 10:11], ALU.mult, ALU.mult)
                        p.inc("V", v.scalar_tensor_tensor(stats[:, 12 + g:13 + g],
                                        stats[:, 7:8], 1.0 / 127.0, smalls[:, 60:61],
                                        ALU.mult, ALU.mult))
                    body("vector", l_v3)
                    lv3 = p.val("V")

                    def l_s3(sc, t=t, tb=tb, g=g, lv3=lv3):
                        p.wait(sc, "V", lv3)
                        p.inc("Sc", sc.activation(xqf[:, :], x1a[tb][:, g * 1024:(g + 1) * 1024],
                                                  AF.Identity, scale=stats[:, 10:11],
                                                  bias=stats[:, 11:12]))
                    body("scalar", l_s3)
                    ls3 = p.val("Sc")

                    def l_v4(v, g=g, ls3=ls3):
                        p.wait(v, "Sc", ls3)
                        v.tensor_scalar(xqf[:, :], xqf[:, :], smalls[:, 42:43], None, ALU.add)
                        v.tensor_scalar(xqf[:, :], xqf[:, :], smalls[:, 42:43], None, ALU.subtract)
                        p.inc("V", v.tensor_copy(xqb[:, g * 1024:(g + 1) * 1024], xqf[:, :]))
                    body("vector", l_v4)
                x1a_used[t] = p.val("V")

                def l_t1(t_, t=t):
                    p.wait(t_, "V", x1a_used[t])
                    t_.transpose(QB[0:1, 0:128], stats[:, 12:13], eyef[:, :])
                    p.inc("T", t_.transpose(QB[0:1, 128:256], stats[:, 13:14], eyef[:, :]))
                body("tensor", l_t1)
                lt1 = p.val("T")

                def l_v5(v, lt1=lt1):
                    p.wait(v, "T", lt1)
                    v.tensor_copy(rowsb[0:1, 0:128], QB[0:1, 0:128])
                    p.inc("V", v.tensor_copy(rowsb[0:1, 128:256], QB[0:1, 128:256]))
                body("vector", l_v5)
                lv5 = p.val("V")

                def l_t2(t_, lv5=lv5):
                    p.wait(t_, "V", lv5)
                    p.inc("T", t_.matmul(QB[:, 0:256], ones1[:, :], rowsb[0:1, 0:256],
                                         start=True, stop=True))
                body("tensor", l_t2)
                lt2 = p.val("T")

                def l_v6(v, lt2=lt2):
                    p.wait(v, "T", lt2)
                    p.inc("V", v.tensor_copy(c1s[:, :], QB[:, 0:256]))
                body("vector", l_v6)
                ln2_done[t] = p.val("V")

                for g in range(2):
                    def l_t3(t_, g=g, t=t):
                        p.wait(t_, "V", ln2_done[t] if g == 0 else p.val("V"))
                        p.wait(t_, "V", pbbf_f[0])
                        for c in range(NCH):
                            inst = t_.transpose(Cbf[:, c * 128:(c + 1) * 128],
                                                xqb[:, g * 1024 + c * 128: g * 1024 + (c + 1) * 128],
                                                eyeb[:, :])
                        p.inc("T", inst)
                    body("tensor", l_t3)
                    lt3 = p.val("T")

                    def l_v7(v, g=g, lt3=lt3):
                        p.wait(v, "T", lt3)
                        xT = xqT[:, :].rearrange("q (c w) -> q c w", w=256)
                        srcp = Cbf[:, :].rearrange("q (c w) -> q c w", w=128)
                        p.inc("V", v.tensor_copy(xT[:, :, g * 128:(g + 1) * 128], srcp))
                    body("vector", l_v7)
                    pbbf_f[0] = p.val("V")
                ln2_xqT[t] = p.val("V")

            pbbf_f = [p.val("V")]
            mm1ev_last = [0]
            conv_done = [0] * NT
            carry_sc = [0] * NT

            def emit_mm1(t):
                c_lo, c_hi, Wt, groups = tile_geom(t)
                off = 1 if t == 0 else 0
                evac_cnt = {}
                for po in range(NHID // 2):
                    o0, o1 = 2 * po, 2 * po + 1
                    bank = Q0 if po % 2 == 0 else Q1

                    def m1(t_, t=t, po=po, o0=o0, o1=o1, bank=bank):
                        if po == 0:
                            p.wait(t_, "V", ln2_xqT[t])
                        if po >= 2:
                            p.wait(t_, "V", evac_cnt[po - 2])
                            p.wait(t_, "Sc", carry_cnt[2 * (po - 2) + 1])
                        elif t > 0:
                            p.wait(t_, "V", mm1ev_last[0])
                            p.wait(t_, "Sc", carry_cnt[2 * (po + 14) + 1])
                        for oo, half in ((o0, 0), (o1, 256)):
                            for k in range(NCH):
                                inst = t_.matmul(bank[:, half:half + 256],
                                                 w1qs[:, k * DH + oo * 128: k * DH + (oo + 1) * 128],
                                                 xqT[:, k * 256:(k + 1) * 256],
                                                 start=(k == 0), stop=(k == NCH - 1))
                        p.inc("T", inst)
                    body("tensor", m1)
                    m1c = p.val("T")

                    for oo, half in ((o0, 0), (o1, 256)):
                        def m1e(v, t=t, oo=oo, half=half, bank=bank, m1c=m1c):
                            p.wait(v, "T", m1c)
                            p.wait(v, "Sc", cts1_cnt[oo - 2] if oo >= 2 else cts1_cnt[oo + 30])
                            v.tensor_tensor(hbuf[oo % 2][:, 2:258],
                                            bank[:, half:half + 256], c1s[:, :], ALU.mult)
                            p.inc("V", v.tensor_copy(hbuf[oo % 2][:, 0:2],
                                                     carries[:, oo * 2:oo * 2 + 2]))
                        body("vector", m1e)
                        evh = p.val("V")

                        def m1c_(sc, oo=oo, evh=evh):
                            p.wait(sc, "V", evh)
                            p.inc("Sc", sc.copy(carries[:, oo * 2:oo * 2 + 2],
                                                hbuf[oo % 2][:, 256:258]))
                        body("scalar", m1c_)
                        carry_cnt[oo] = p.val("Sc")

                        def cts1(sc, t=t, oo=oo, off=off, Wt=Wt):
                            p.wait(sc, "V", stt2_cnt[oo - 2] if oo >= 2 else stt2_cnt[oo + 30])
                            p.inc("Sc", sc.activation(cv1[oo % 2][:, 0:Wt],
                                                      hbuf[oo % 2][:, off + 2:off + 2 + Wt],
                                                      AF.Identity,
                                                      scale=convw_s[:, 64 + oo:65 + oo],
                                                      bias=convb_s[:, oo:oo + 1]))
                        body("scalar", cts1)
                        cts1c = p.val("Sc")
                        cts1_cnt[oo] = cts1c

                        def cstt1(v, oo=oo, off=off, Wt=Wt, cts1c=cts1c):
                            p.wait(v, "Sc", cts1c)
                            v.scalar_tensor_tensor(cv1[oo % 2][:, 0:Wt],
                                                   hbuf[oo % 2][:, off + 1:off + 1 + Wt],
                                                   convw_s[:, 32 + oo:33 + oo],
                                                   cv1[oo % 2][:, 0:Wt], ALU.mult, ALU.add)
                            p.inc("V", v.scalar_tensor_tensor(hsil[:, oo * 260:oo * 260 + Wt],
                                                              hbuf[oo % 2][:, off:off + Wt],
                                                              convw_s[:, oo:oo + 1],
                                                              cv1[oo % 2][:, 0:Wt],
                                                              ALU.mult, ALU.add))
                        body("vector", cstt1)
                        stt2_cnt[oo] = p.val("V")
                    evac_cnt[po] = p.val("V")
                mm1ev_last[0] = p.val("V")
                mm1_done[t] = p.val("T")
                conv_done[t] = p.val("V")


            carry_cnt = [0] * NHID
            stt2_cnt = [0] * NHID
            cts1_cnt = [0] * NHID
            ln2_xqT = [0] * (NT + 1)

            def emit_carry_fill(t):
                pass

            def emit_quant2(t):
                c_lo, c_hi, Wt, groups = tile_geom(t)

                def qs_silu(sc, t=t, Wt=Wt):
                    p.wait(sc, "V", conv_done[t])
                    hs_c = hsil[:, :].rearrange("q (c w) -> q c w", w=260)[:, :, 0:Wt]
                    p.inc("Sc", sc.activation(hs_c, hs_c, AF.Silu))
                body("scalar", qs_silu)
                silu_c = p.val("Sc")
                if qstage < 2:
                    return

                if dump_hsil and t == NT - 1:
                    def qdmp(sc, silu_c=silu_c, Wt=Wt):
                        p.wait(sc, "Sc", silu_c)
                        dv = dbg_h.rearrange("q (c w) -> q c w", w=260)[:, :, 0:Wt]
                        hv = hsil[:, :].rearrange("q (c w) -> q c w", w=260)[:, :, 0:Wt]
                        p.inc("dF", sc.dma_start(dv, hv), 16)
                    body("scalar", qdmp)

                def qv_m0(v, silu_c=silu_c, Wt=Wt):
                    p.wait(v, "Sc", silu_c)
                    p.inc("V", v.memset(mmap[:, :], 0.0))
                body("vector", qv_m0)

                sq_cnt = {}
                mx_cnt = {}
                for o in range(NHID):
                    def qv_mx(v, o=o, Wt=Wt, silu_c=silu_c):
                        if o == 0:
                            p.wait(v, "Sc", silu_c)
                        hc = hsil[:, o * 260:o * 260 + Wt]
                        v.tensor_tensor(mmap2[:, (o % 2) * 260:(o % 2) * 260 + Wt],
                                        hc, hc, ALU.mult)
                        p.inc("V", v.tensor_tensor(mmap[:, 0:Wt], mmap[:, 0:Wt],
                                                   mmap2[:, (o % 2) * 260:(o % 2) * 260 + Wt],
                                                   ALU.max))
                    body("vector", qv_mx)
                    mx_cnt[o] = p.val("V")
                am_c = p.val("V")
                if qstage < 3:
                    return

                npieces = (Wt + 127) // 128
                qvr_cnt = {}
                qb_read = [am_c]
                for pi in range(npieces):
                    wp = min(128, Wt - pi * 128)
                    rwp = min(128, 260 - pi * 128)

                    def qt_m(t_, pi=pi, rwp=rwp):
                        p.wait(t_, "V", qb_read[0])
                        p.inc("T", t_.transpose(QB[0:rwp, 256 + (pi % 2) * 128:256 + (pi % 2) * 128 + 128],
                                                mmap[:, pi * 128:pi * 128 + rwp], eyef[:, :]))
                    body("tensor", qt_m)
                    qtm = p.val("T")

                    def qv_r0(v, pi=pi, rwp=rwp, qtm=qtm):
                        p.wait(v, "T", qtm)
                        bounce = mmap2[:, (pi % 2) * 260:(pi % 2) * 260 + 128]
                        p.flush(v, "V", v.tensor_copy(
                            bounce[0:rwp, :],
                            QB[0:rwp, 256 + (pi % 2) * 128:256 + (pi % 2) * 128 + 128]))
                        p.inc("V", v.reduce_max(stats[0:rwp, 16 + pi:17 + pi].unsqueeze(2),
                                     bounce[0:rwp, :].unsqueeze(1), AX.X))
                    body("vector", qv_r0)
                    qv0 = p.val("V")
                    qb_read[0] = qv0
                    if qstage == 11:
                        continue

                    def qs_r(sc, pi=pi, wp=rwp, qv0=qv0):
                        p.wait(sc, "V", qv0)
                        p.inc("Sc", sc.activation(stats[0:wp, 16 + pi:17 + pi],
                                                  stats[0:wp, 16 + pi:17 + pi], AF.Sqrt))
                    body("scalar", qs_r)
                    qsr = p.val("Sc")
                    if qstage == 12:
                        continue

                    def qv_r(v, pi=pi, wp=rwp, t=t, qsr=qsr):
                        p.wait(v, "Sc", qsr)
                        p.wait(v, "V")
                        p.flush(v, "V", v.tensor_scalar(stats[0:wp, 16 + pi:17 + pi],
                                        stats[0:wp, 16 + pi:17 + pi],
                                        1e-5, None, ALU.max))
                        p.flush(v, "V", v.reciprocal(stats[0:wp, 20 + pi:21 + pi],
                                        stats[0:wp, 16 + pi:17 + pi]))
                        v.tensor_scalar(stats[0:wp, 20 + pi:21 + pi], stats[0:wp, 20 + pi:21 + pi],
                                        127.0, None, ALU.mult)
                        cc2 = 48 + (t % 2) * 4 + pi
                        p.inc("V", v.scalar_tensor_tensor(stats[0:wp, cc2:cc2 + 1],
                                        stats[0:wp, 16 + pi:17 + pi], 1.0 / 127.0,
                                        smalls[:, 58:59][0:wp, :], ALU.mult, ALU.mult))
                    body("vector", qv_r)
                    qvr_cnt[pi] = p.val("V")
                s2_c = p.val("V")
                if qstage < 4 or qstage in (10, 11, 12, 13, 21, 22, 23, 24, 25, 26):
                    return

                qrr_cnt = {}
                for pi in range(npieces):
                    wp = min(128, Wt - pi * 128)

                    def qt_r(t_, pi=pi, wp=wp, s2_c=s2_c, t=t):
                        p.wait(t_, "V", s2_c)
                        p.wait(t_, "V", qb_read[0])
                        if pi == 0:
                            p.wait(t_, "Sc", mm2ev_c[0])
                        p.inc("T", t_.transpose(QB[0:1, 256 + (pi % 2) * 128: 256 + (pi % 2) * 128 + wp],
                                                stats[0:wp, 20 + pi:21 + pi], eyef[0:wp, 0:wp]))
                    body("tensor", qt_r)
                    qtr = p.val("T")

                    def qv_rr(v, pi=pi, wp=wp, qtr=qtr):
                        p.wait(v, "T", qtr)
                        p.inc("V", v.tensor_copy(rowsb[0:1, pi * 128:pi * 128 + wp],
                                                 QB[0:1, 256 + (pi % 2) * 128:256 + (pi % 2) * 128 + wp]))
                    body("vector", qv_rr)
                    qrr_cnt[pi] = p.val("V")
                    qb_read[0] = qrr_cnt[pi]
                rr_c = p.val("V")

                def qt_bc(t_, Wt=Wt, rr_c=rr_c):
                    p.wait(t_, "V", rr_c)
                    p.inc("T", t_.matmul(C0[:, 0:Wt], ones1[:, :], rowsb[0:1, 0:Wt],
                                         start=True, stop=True))
                body("tensor", qt_bc)
                bc_c = p.val("T")
                if qstage < 5:
                    return

                def qv_rd(v, Wt=Wt, bc_c=bc_c, t=t):
                    p.wait(v, "T", bc_c)
                    if dump_hsil and t == NT - 1:
                        p.wait(v, "dF")
                    hs_c = hsil[:, :].rearrange("q (c w) -> q c w", w=260)[:, :, 0:Wt]
                    s2b = C0[:, 0:Wt].unsqueeze(1).broadcast_to([128, NHID, Wt])
                    p.inc("V", v.tensor_tensor(hs_c, hs_c, s2b, ALU.mult))
                body("vector", qv_rd)
                rd1 = p.val("V")
                if qstage < 6:
                    return

                def qg_rd(v, Wt=Wt, rd1=rd1):
                    hs_c = hsil[:, :].rearrange("q (c w) -> q c w", w=260)[:, :, 0:Wt]
                    v.tensor_scalar(hs_c, hs_c, smalls[:, 42:43], None, ALU.add)
                    p.inc("V", v.tensor_scalar(hs_c, hs_c, smalls[:, 42:43], None, ALU.subtract))
                body("vector", qg_rd)
                rd2 = p.val("V")
                if qstage < 7:
                    return

                def qv_rd2(v, Wt=Wt, rd2=rd2, t=t):
                    if t >= 1:
                        p.wait(v, "T", mm2_done[t - 1])
                    hs_c = hsil[:, :].rearrange("q (c w) -> q c w", w=260)[:, :, 0:Wt]
                    hq_c = hq[:, :].rearrange("q (c w) -> q c w", w=260)[:, :, 0:Wt]
                    p.inc("V", v.tensor_copy(hq_c, hs_c))
                body("vector", qv_rd2)
                hq_done[t] = p.val("V")

            mm2ev_c = [0]

            def emit_mm2(t):
                c_lo, c_hi, Wt, groups = tile_geom(t)
                gb = {}
                for gi in range(len(groups)):
                    gb[gi] = [C0, C1][gi] if gi < 2 else None

                piece_c = [0] * 9
                for j in range(8):
                    def w2_ld(sync, t=t, j=j):
                        if not (t == 0 and j < 2):
                            p.wait(sync, "T", piece_c[j - 2] if j >= 2 else mm2_prev_pc[0])
                            for c in range(4):
                                p.inc("dF", sync.dma_start(
                                    w2p[j % 2][:, c * D:(c + 1) * D],
                                    w2q_scr[(j * 4 + c) * 128:(j * 4 + c + 1) * 128, :]), 16)
                    body("sync", w2_ld)
                    ldF = p.val("dF")

                    def m2(t_, t=t, j=j, groups=groups, c_lo=c_lo, ldF=ldF):
                        p.wait(t_, "dF", ldF)
                        if j == 0:
                            p.wait(t_, "V", hq_done[t])
                        for kk in range(4):
                            k = j * 4 + kk
                            for gi, (goff, gsz) in enumerate(groups):
                                lhs_src = hq[:, :].rearrange("q (c w) -> q c w", w=260)
                                lhs = lhs_src[:, k, goff:goff + gsz]
                                if gi < 2:
                                    bank = [C0, C1][gi]
                                    t_.matmul(bank[0:gsz, 0:512], lhs,
                                              w2p[j % 2][:, kk * D:kk * D + 512],
                                              start=(k == 0), stop=(k == DH // 128 - 1))
                                    inst = t_.matmul(bank[0:gsz, 512:1024], lhs,
                                                     w2p[j % 2][:, kk * D + 512:(kk + 1) * D],
                                                     start=(k == 0), stop=(k == DH // 128 - 1))
                                else:
                                    t_.matmul(Q0[0:gsz, 0:512], lhs,
                                              w2p[j % 2][:, kk * D:kk * D + 512],
                                              start=(k == 0), stop=(k == DH // 128 - 1))
                                    inst = t_.matmul(Q1[0:gsz, 0:512], lhs,
                                                     w2p[j % 2][:, kk * D + 512:(kk + 1) * D],
                                                     start=(k == 0), stop=(k == DH // 128 - 1))
                        p.inc("T", inst)
                    body("tensor", m2)
                    piece_c[j] = p.val("T")
                mm2_done[t] = p.val("T")
                mm2_prev_pc[0] = mm2_done[t]

                for gi, (goff, gsz) in enumerate(groups):
                    def r_ld(sync, t=t, gi=gi, goff=goff, gsz=gsz, c_lo=c_lo):
                        oi = out_idx[0] + gi
                        p.wait(sync, f"dO{oi % 2}")
                        p.inc("dF", sync.dma_start(
                            x1s[oi % 2][0:gsz, :],
                            x1_scr[c_lo + goff: c_lo + goff + gsz, :]), 16)
                    body("sync", r_ld)
                    rld = p.val("dF")

                    def r_ev(sc, t=t, gi=gi, gsz=gsz, rld=rld):
                        p.wait(sc, "T", mm2_done[t])
                        if dump_hsil and t == NT - 1 and gi == 1:
                            sc.copy(fdump[0:gsz, :].bitcast(F32) if False else ao2[0:gsz, :],
                                    C1[0:gsz, :])
                            p.inc("dF", sc.dma_start(dbg_p[0:gsz, :], ao2[0:gsz, :]), 16)
                        if gi < 2:
                            bank = [C0, C1][gi]
                            inst = sc.activation(ao2[0:gsz, :], bank[0:gsz, :], AF.Identity,
                                                 scale=stats[0:gsz, 48 + (t % 2) * 4 + gi:49 + (t % 2) * 4 + gi])
                        else:
                            sc.activation(ao2[0:gsz, 0:512], Q0[0:gsz, 0:512], AF.Identity,
                                          scale=stats[0:gsz, 48 + (t % 2) * 4 + gi:49 + (t % 2) * 4 + gi])
                            inst = sc.activation(ao2[0:gsz, 512:1024], Q1[0:gsz, 0:512],
                                                 AF.Identity, scale=stats[0:gsz, 48 + (t % 2) * 4 + gi:49 + (t % 2) * 4 + gi])
                        p.inc("Sc", sc.activation(fdump[0:gsz, :], ao2[0:gsz, :], AF.Square,
                                                  accum_out=stats[0:gsz, 28:29]))
                    body("scalar", r_ev)
                    rev = p.val("Sc")

                    def r_v1(v, gsz=gsz, rev=rev):
                        p.wait(v, "Sc", rev)
                        p.flush(v, "V", v.reduce_sum(stats[0:gsz, 29:30].unsqueeze(2),
                                     ao2[0:gsz, :].unsqueeze(1), AX.X))
                        v.tensor_scalar(stats[0:gsz, 30:31], stats[0:gsz, 29:30], 1.0 / D,
                                        None, ALU.mult)
                        p.flush(v, "V", v.tensor_scalar(stats[0:gsz, 31:32], stats[0:gsz, 28:29],
                                        1.0 / D, EPS_LN, ALU.mult, ALU.add))
                        p.flush(v, "V", v.tensor_mul(stats[0:gsz, 32:33], stats[0:gsz, 30:31],
                                        stats[0:gsz, 30:31]))
                        p.flush(v, "V", v.tensor_sub(stats[0:gsz, 31:32], stats[0:gsz, 31:32],
                                        stats[0:gsz, 32:33]))
                        p.inc("V", v.reciprocal(stats[0:gsz, 31:32], stats[0:gsz, 31:32]))
                    body("vector", r_v1)
                    rv1 = p.val("V")

                    def r_s2(sc, gsz=gsz, rv1=rv1):
                        p.wait(sc, "V", rv1)
                        p.inc("Sc", sc.activation(stats[0:gsz, 31:32], stats[0:gsz, 31:32],
                                                  AF.Sqrt))
                    body("scalar", r_s2)
                    rs2 = p.val("Sc")

                    def r_v2(v, gsz=gsz, rs2=rs2):
                        p.wait(v, "Sc", rs2)
                        p.wait(v, "V")
                        p.inc("V", v.scalar_tensor_tensor(stats[0:gsz, 33:34],
                                        stats[0:gsz, 30:31], -1.0, stats[0:gsz, 31:32],
                                        ALU.mult, ALU.mult))
                    body("vector", r_v2)
                    rv2 = p.val("V")

                    def r_s3(sc, gi=gi, gsz=gsz, rv2=rv2, rld=rld):
                        p.wait(sc, "V", rv2)
                        p.wait(sc, "dF", rld)
                        oi = out_idx[0] + gi
                        p.inc("Sc", sc.activation(x1s[oi % 2][0:gsz, :], x1s[oi % 2][0:gsz, :],
                                                  AF.Identity, bias=stats[0:gsz, 33:34]))
                    body("scalar", r_s3)
                    rs3 = p.val("Sc")

                    def r_v3(v, gi=gi, gsz=gsz, rs3=rs3):
                        p.wait(v, "Sc", rs3)
                        oi = out_idx[0] + gi
                        p.inc("V", v.scalar_tensor_tensor(outb[oi % 2][0:gsz, :], ao2[0:gsz, :],
                                                          stats[0:gsz, 31:32],
                                                          x1s[oi % 2][0:gsz, :],
                                                          ALU.mult, ALU.add))
                    body("vector", r_v3)
                    rv3 = p.val("V")

                    def r_st(sc, t=t, gi=gi, goff=goff, gsz=gsz, c_lo=c_lo, rv3=rv3):
                        p.wait(sc, "V", rv3)
                        oi = out_idx[0] + gi
                        p.inc(f"dO{oi % 2}", sc.dma_start(
                            out[c_lo + goff: c_lo + goff + gsz, :], outb[oi % 2][0:gsz, :]), 16)
                    body("scalar", r_st)
                out_idx[0] += len(groups)
                mm2ev_c[0] = p.val("Sc")

            mm2_prev_pc = [0]


            fs = [0]

            def go():
                fs[0] += 1
                return fs[0] <= fstage

            if go():
                emit_ln2(0)
            if go():
                emit_mm1(0)
            if go():
                emit_ln2(1)
            if go():
                emit_quant2(0)
            for t in range(1, NT):
                if go():
                    emit_mm1(t)
                if go():
                    emit_mm2(t - 1)
                if go():
                    emit_quant2(t)
                if t + 1 < NT:
                    if go():
                        emit_ln2(t + 1)
            if go():
                emit_mm2(NT - 1)

            def fin(sc):
                p.wait(sc, "dO0")
                p.wait(sc, "dO1")
            body("scalar", fin)


    return nc


def make_inputs(core_x, ins):
    d = {}
    d["x"] = np.ascontiguousarray(core_x.astype(np.float32, copy=False))
    for m, key in [("r", "L0_attn_w_r"), ("k", "L0_attn_w_k"), ("v", "L0_attn_w_v"),
                   ("g", "L0_attn_w_g"), ("o", "L0_attn_w_o")]:
        d[f"w{m}T"] = np.ascontiguousarray(np.asarray(ins[key]).T)
    d["w1T"] = np.ascontiguousarray(np.asarray(ins["L0_ffn_w1"]).T)
    d["w2T"] = np.ascontiguousarray(np.asarray(ins["L0_ffn_w2"]).T)
    maa = np.stack([np.asarray(ins[f"L0_attn_time_maa_{m}"]) for m in "rkvg"], 0)
    d["maa"] = np.ascontiguousarray(maa.reshape(4, 8, 128).transpose(2, 0, 1).reshape(128, 32))
    decay = np.asarray(ins["L0_attn_time_decay"])
    d["lnwbc"] = np.ascontiguousarray(np.broadcast_to(np.repeat(decay, HN)[None, :], (128, D)))
    d["decay_b"] = np.ascontiguousarray(np.broadcast_to(decay[None, :], (128, H)))
    d["faaaa_b"] = np.ascontiguousarray(
        np.broadcast_to(np.asarray(ins["L0_attn_time_faaaa"])[None, :], (128, H)))
    d["decay_s"] = np.ascontiguousarray(np.broadcast_to(decay[None, :], (128, H)).astype(np.float32))
    d["lng_bc"] = np.ascontiguousarray(
        np.broadcast_to(np.asarray(ins["L0_attn_ln_g"])[None, :], (128, D)))
    d["lnb_bc"] = np.ascontiguousarray(
        np.broadcast_to(np.asarray(ins["L0_attn_ln_b"])[None, :], (128, D)))
    cw = np.asarray(ins["L0_ffn_conv_w"]).reshape(DH, 3)
    d["convw"] = np.ascontiguousarray(
        cw.reshape(NHID, 128, 3).transpose(1, 2, 0).reshape(128, 3 * NHID))
    d["convb"] = np.ascontiguousarray(np.asarray(ins["L0_ffn_conv_b"]).reshape(NHID, 128).T)
    return d


def kernel(**inputs):
    x = np.asarray(inputs["L0_attn_x"])
    B, T, _ = x.shape
    nc = build_program(T=T, stub_ffn=False)
    in_maps = [make_inputs(x[b], inputs) for b in range(B)]
    res = run_bass_kernel_spmd(nc, in_maps, core_ids=list(range(B)))
    return np.stack([res.results[b]["out"] for b in range(B)], 0)


# version 06
# speedup vs baseline: 1.0222x; 1.0222x over previous
import contextlib
import numpy as np
import ml_dtypes
import concourse.bass as bass
import concourse.mybir as mybir
from concourse.bass_utils import run_bass_kernel_spmd

F32 = mybir.dt.float32
BF16 = mybir.dt.bfloat16
I32 = mybir.dt.int32
AF = mybir.ActivationFunctionType
ALU = mybir.AluOpType
AX = mybir.AxisListType

D = 1024
H = 16
HN = 64
DH = 4096
NCH = 8
NHID = 32
EPS_LN = 1e-5
EPS_GN = 64e-5


class Prog:
    def __init__(self, nc, stack):
        self.nc = nc
        self.stack = stack
        self.sems = {}
        self.cnt = {}

    def sem(self, name):
        self.sems[name] = self.stack.enter_context(self.nc.semaphore(name))
        self.cnt[name] = 0

    def inc(self, name, inst, k=1):
        inst.then_inc(self.sems[name], k)
        self.cnt[name] += k

    def wait(self, eng, name, val=None):
        v = self.cnt[name] if val is None else val
        if v > 0:
            eng.wait_ge(self.sems[name], v)

    def val(self, name):
        return self.cnt[name]

    def flush(self, eng, name, inst):
        self.inc(name, inst)
        eng.wait_ge(self.sems[name], self.cnt[name])


def build_program(T=4096, dump_x1=False, stub_ffn=False, dump_hsil=False, astage=99, dump_attn=False, fstage=99, qstage=99):
    nc = bass.Bass("TRN2", target_bir_lowering=False, detect_race_conditions=False)
    S = T // 256
    assert T % 256 == 0

    x_in = nc.dram_tensor("x", [T, D], F32, kind="ExternalInput").ap()
    wT = {m: nc.dram_tensor(f"w{m}T", [D, D], F32, kind="ExternalInput").ap() for m in "rkvgo"}
    w1T = nc.dram_tensor("w1T", [D, DH], F32, kind="ExternalInput").ap()
    w2T = nc.dram_tensor("w2T", [DH, D], F32, kind="ExternalInput").ap()
    maa_in = nc.dram_tensor("maa", [128, 32], F32, kind="ExternalInput").ap()
    lnw_in = nc.dram_tensor("lnwbc", [128, D], F32, kind="ExternalInput").ap()
    decay_b = nc.dram_tensor("decay_b", [128, H], F32, kind="ExternalInput").ap()
    faaaa_b = nc.dram_tensor("faaaa_b", [128, H], F32, kind="ExternalInput").ap()
    decay_s = nc.dram_tensor("decay_s", [128, 16], F32, kind="ExternalInput").ap()
    lng_bc = nc.dram_tensor("lng_bc", [128, D], F32, kind="ExternalInput").ap()
    lnb_bc = nc.dram_tensor("lnb_bc", [128, D], F32, kind="ExternalInput").ap()
    convw_in = nc.dram_tensor("convw", [128, 3 * NHID], F32, kind="ExternalInput").ap()
    convb_in = nc.dram_tensor("convb", [128, NHID], F32, kind="ExternalInput").ap()

    out = nc.dram_tensor("out", [T, D], F32, kind="ExternalOutput").ap()
    x1_scr = nc.dram_tensor("x1_scr", [T, D], F32,
                            kind="ExternalOutput" if dump_x1 else "Internal").ap()
    dbg_h = nc.dram_tensor("dbg_h", [128, 32 * 260], F32,
                           kind="ExternalOutput" if dump_hsil else "Internal").ap()
    dbg_p = nc.dram_tensor("dbg_p", [128, 1024], F32,
                           kind="ExternalOutput" if dump_hsil else "Internal").ap()
    dbg_ab = nc.dram_tensor("dbg_ab", [128, 10240], BF16,
                            kind="ExternalOutput" if dump_attn else "Internal").ap()
    dbg_af = nc.dram_tensor("dbg_af", [128, 2048], F32,
                            kind="ExternalOutput" if dump_attn else "Internal").ap()
    w1q_scr = nc.dram_tensor("w1q_scr", [D, DH], BF16, kind="Internal").ap()
    w2q_scr = nc.dram_tensor("w2q_scr", [DH, D], BF16, kind="Internal").ap()

    eyeb_h = nc.inline_tensor(np.eye(128).astype(ml_dtypes.bfloat16), "eyeb_h").ap()
    eyef_h = nc.inline_tensor(np.eye(128, dtype=np.float32), "eyef_h").ap()
    iota_r = nc.inline_tensor(np.arange(128, dtype=np.float32).reshape(128, 1), "iota_r").ap()
    iota_k = nc.inline_tensor((-(np.arange(128, dtype=np.float32) + 1.0)).reshape(128, 1),
                              "iota_k").ap()
    jj, ii = np.meshgrid(np.arange(128), np.arange(128), indexing="ij")
    u01_h = nc.inline_tensor((jj < ii).astype(np.float32), "u01_h").ap()
    ones1_h = nc.inline_tensor(np.ones((1, 128), dtype=np.float32), "ones1_h").ap()

    stack = contextlib.ExitStack()
    p = Prog(nc, stack)

    with stack, nc.Block() as blk:
        for sn in ["dC", "dX", "dW", "dSx0", "dSx1", "dQ0", "dQ1", "dF",
                   "dL0", "dL1", "dF0", "dF1", "dR0", "dR1",
                   "dO0", "dO1", "T", "V", "Sc", "G", "bar"]:
            p.sem(sn)

        def body(engname, fn):
            getattr(blk, engname)(fn)

        def alloc(st, name, shape, dtype=F32):
            return st.enter_context(nc.sbuf_tensor(name, shape, dtype))

        def palloc(st, name, shape, dtype=F32):
            return st.enter_context(nc.psum_tensor(name, shape, dtype))

        per = contextlib.ExitStack(); stack.enter_context(per)
        maa = alloc(per, "maa_s", [128, 32])
        eyeb = alloc(per, "eyeb", [128, 128], BF16)
        eyef = alloc(per, "eyef", [128, 128])
        ones1 = alloc(per, "ones1", [1, 128])
        stats = alloc(per, "stats", [128, 64])
        gstats = alloc(per, "gstats", [128, 48])
        smalls = alloc(per, "smalls", [128, 64])
        onecol = alloc(per, "onecol", [128, 8])
        convw_s = alloc(per, "convw_s", [128, 3 * NHID])
        convb_s = alloc(per, "convb_s", [128, NHID])

        att = contextlib.ExitStack(); stack.enter_context(att)
        WTs = {m: alloc(att, f"WT{m}", [128, NCH * D], BF16) for m in "rkvgo"}
        xbuf = [alloc(att, f"xbuf{i}", [128, 2048]) for i in range(2)]
        ln0tm = alloc(att, "ln0tm", [128, 2048], BF16)
        ln0T = alloc(att, "ln0T", [128, NCH * 257], BF16)
        xxb = alloc(att, "xxb", [128, 2048], BF16)
        mixb = [alloc(att, f"mixb{i}", [128, 2048], BF16) for i in range(2)]
        rtm = alloc(att, "rtm", [128, 2048], BF16)
        ktok = alloc(att, "ktok", [128, 2048], BF16)
        vtm = alloc(att, "vtm", [128, 2048], BF16)
        gtm = alloc(att, "gtm", [128, 2048], BF16)
        rfm = alloc(att, "rfm64", [64, 4096], BF16)
        kfm = alloc(att, "kfm64", [64, 4096], BF16)
        deckR = alloc(att, "deckR", [128, D], BF16)
        deckK = alloc(att, "deckK", [128, D], BF16)
        wmask = alloc(att, "wmask", [128, H * 128], BF16)
        aring = alloc(att, "aring", [128, 4 * 128], BF16)
        yfm = alloc(att, "yfm", [128, D])
        ytm = alloc(att, "ytm", [128, D])
        ytotm = alloc(att, "ytotm", [128, D], BF16)
        ytotT = alloc(att, "ytotT", [128, 2048], BF16)
        aot = alloc(att, "aot", [128, D])
        sqdmp = alloc(att, "sqdmp", [128, 1024], BF16)
        sS = alloc(att, "sS", [64, 1024])
        sSb = alloc(att, "sSb", [64, 1024], BF16)
        wQt = alloc(att, "wQt", [128, 16])
        lngb = alloc(att, "lngb", [128, D])
        lnbb = alloc(att, "lnbb", [128, D])
        wstg = [alloc(att, f"wstg{i}", [128, 1024]) for i in range(2)]
        lnwbc = aot
        wqbf = [alloc(att, f"wqbf{i}", [128, 1024], BF16) for i in range(2)]

        patt = contextlib.ExitStack(); stack.enter_context(patt)
        PB = [palloc(patt, f"PB{i}", [128, 512]) for i in range(4)]
        PBbf = palloc(patt, "PBbf", [128, 1024], BF16)
        PY = palloc(patt, "PY", [128, 512])
        PA = palloc(patt, "PA", [128, 512])
        PT = palloc(patt, "PT", [64, 512])

        def pro_sync(sync):
            loads = [(maa[:, :], maa_in), (eyeb[:, :], eyeb_h), (eyef[:, :], eyef_h),
                     (ones1[:, :], ones1_h), (lnwbc[:, :], lnw_in),
                     (stats[:, 0:H], decay_b), (smalls[:, 16:32], faaaa_b),
                     (gstats[:, 0:16], decay_s),
                     (lngb[:, :], lng_bc), (lnbb[:, :], lnb_bc),
                     (convw_s[:, :], convw_in), (convb_s[:, :], convb_in),
                     (wstg[0][:, 0:128], u01_h),
                     (smalls[:, 40:41], iota_r), (smalls[:, 41:42], iota_k)]
            for d_, s_ in loads:
                p.inc("dC", sync.dma_start(d_, s_), 16)
        body("sync", pro_sync)
        pro_dY = p.val("dC")

        def pro_scal(sc):
            p.wait(sc, "dC", pro_dY)
            p.flush(sc, "Sc", sc.activation(lnwbc[:, :], lnwbc[:, :], AF.Exp))
            p.flush(sc, "Sc", sc.activation(lnwbc[:, :], lnwbc[:, :], AF.Identity, scale=-1.0))
            p.flush(sc, "Sc", sc.activation(stats[:, 0:H], stats[:, 0:H], AF.Exp))
            p.flush(sc, "Sc", sc.activation(stats[:, 0:H], stats[:, 0:H], AF.Identity,
                                            scale=-1.0))
            p.flush(sc, "Sc", sc.activation(stats[:, 0:H], stats[:, 0:H], AF.Exp))
            p.flush(sc, "Sc", sc.activation(gstats[:, 0:16], gstats[:, 0:16], AF.Exp))
            p.flush(sc, "Sc", sc.activation(gstats[:, 0:16], gstats[:, 0:16], AF.Identity,
                                            scale=-128.0))
            p.inc("Sc", sc.activation(wQt[:, :], gstats[:, 0:16], AF.Exp))
        body("scalar", pro_scal)
        pro_sc = p.val("Sc")

        def pro_vec(v):
            p.wait(v, "Sc", pro_sc)
            p.flush(v, "V", v.tensor_mul(smalls[:, 16:32], smalls[:, 16:32],
                                         stats[:, 0:H]))
            for h in range(H):
                v.scalar_tensor_tensor(wmask[:, h * 128:(h + 1) * 128], eyef[:, :],
                                       smalls[:, 16 + h:17 + h], wstg[0][:, 0:128],
                                       ALU.mult, ALU.add)
            v.memset(sS[0:64, :], 0.0)
            v.memset(sSb[0:64, :], 0.0)
            v.memset(onecol[:, 2:3], 1.0)
            ln0T_c = ln0T[:, :].rearrange("q (c w) -> q c w", w=257)
            p.inc("V", v.memset(ln0T_c[:, :, 0:1], 0.0))
        body("vector", pro_vec)
        pro_v = p.val("V")

        def magic_v(v):
            p.flush(v, "V", v.memset(smalls[:, 43:44], 1.0))
            p.flush(v, "V", v.tensor_scalar(smalls[:, 44:45], smalls[:, 43:44], 2.0 ** -29,
                                            None, ALU.add))
            p.flush(v, "V", v.tensor_scalar(smalls[:, 44:45], smalls[:, 44:45], 1.0, None,
                                            ALU.subtract))
            p.flush(v, "V", v.tensor_scalar(smalls[:, 44:45], smalls[:, 44:45], 2.0 ** 29,
                                            None, ALU.mult))
            v.tensor_scalar(smalls[:, 42:43], smalls[:, 44:45],
                            (1.5 * 2.0 ** 52) - (1.5 * 2.0 ** 23), 1.5 * 2.0 ** 23,
                            ALU.mult, ALU.add)
            p.inc("V", v.memset(smalls[:, 45:46], 0.0))
        body("vector", magic_v)

        def deck_scal(sc):
            p.wait(sc, "V", pro_v)
            p.wait(sc, "Sc")
            sc.activation(deckK[:, :], lnwbc[:, :], AF.Exp, scale=smalls[:, 41:42])
            p.inc("Sc", sc.activation(deckR[:, :], lnwbc[:, :], AF.Exp, scale=smalls[:, 40:41]))
        body("scalar", deck_scal)

        NW = 5 * NCH
        wl_casts = []
        for i in range(NW):
            m = "rkvgo"[i // NCH]; k = i % NCH

            def wl_s(sync, m=m, k=k, i=i):
                p.wait(sync, "V", wl_casts[i - 2] if i >= 2 else p.val("V"))
                p.inc("dW", sync.dma_start(wstg[i % 2][:, :],
                                           wT[m][k * 128:(k + 1) * 128, :]), 16)
            body("sync", wl_s)

            def wl_v(v, m=m, k=k, i=i):
                p.wait(v, "dW")
                p.inc("V", v.tensor_copy(WTs[m][:, k * D:(k + 1) * D], wstg[i % 2][:, :]))
            body("vector", wl_v)
            wl_casts.append(p.val("V"))

        SUMC, SQC, MUC, E2C, M2C, VARC, RSTDC, NBC = 0, 2, 4, 6, 8, 10, 12, 14
        x1_store_cnt = [0] * S
        pbbf_free = p.val("V")
        pb_free = [p.val("Sc") if False else 0] * 8
        last_amask = 0
        prev_ao_sc = 0

        for s in range(S):
            sb = s % 2

            def ld_x(sync, s=s, sb=sb):
                if s >= 2:
                    p.wait(sync, f"dSx{sb}", x1_store_cnt[s - 2])
                for g in range(2):
                    p.inc("dX", sync.dma_start(
                        xbuf[sb][:, g * 1024:(g + 1) * 1024],
                        x_in[s * 256 + g * 128: s * 256 + (g + 1) * 128, :]), 16)
            body("sync", ld_x)
            xload = p.val("dX")

            if astage >= 1:
                def ln0_v1(v, sb=sb, xload=xload):
                    p.wait(v, "dX", xload)
                    for g in range(2):
                        p.inc("V", v.reduce_sum(stats[:, SUMC + g:SUMC + g + 1].unsqueeze(2),
                                                xbuf[sb][:, g * 1024:(g + 1) * 1024].unsqueeze(1),
                                                AX.X))
                body("vector", ln0_v1)
    
                def ln0_s1(sc, sb=sb, xload=xload):
                    p.wait(sc, "dX", xload)
                    p.wait(sc, "T")
                    for g in range(2):
                        p.inc("Sc", sc.activation(sqdmp[:, :], xbuf[sb][:, g * 1024:(g + 1) * 1024],
                                                  AF.Square,
                                                  accum_out=stats[:, SQC + g:SQC + g + 1]))
                body("scalar", ln0_s1)
                sq_c = p.val("Sc")
    
                def ln0_v2(v, sq_c=sq_c):
                    p.wait(v, "Sc", sq_c)
                    p.wait(v, "V")
                    for g in range(2):
                        v.tensor_scalar(stats[:, MUC + g:MUC + g + 1],
                                        stats[:, SUMC + g:SUMC + g + 1], 1.0 / D, None, ALU.mult)
                        p.flush(v, "V", v.tensor_scalar(stats[:, E2C + g:E2C + g + 1],
                                        stats[:, SQC + g:SQC + g + 1], 1.0 / D, EPS_LN,
                                        ALU.mult, ALU.add))
                        p.flush(v, "V", v.tensor_mul(stats[:, M2C + g:M2C + g + 1],
                                     stats[:, MUC + g:MUC + g + 1],
                                     stats[:, MUC + g:MUC + g + 1]))
                        p.flush(v, "V", v.tensor_sub(stats[:, VARC + g:VARC + g + 1],
                                     stats[:, E2C + g:E2C + g + 1],
                                     stats[:, M2C + g:M2C + g + 1]))
                        p.inc("V", v.reciprocal(stats[:, VARC + g:VARC + g + 1],
                                                stats[:, VARC + g:VARC + g + 1]))
                body("vector", ln0_v2)
                lv2 = p.val("V")
    
                def ln0_s2(sc, lv2=lv2):
                    p.wait(sc, "V", lv2)
                    for g in range(2):
                        inst = sc.activation(stats[:, RSTDC + g:RSTDC + g + 1],
                                             stats[:, VARC + g:VARC + g + 1], AF.Sqrt)
                    p.inc("Sc", inst)
                body("scalar", ln0_s2)
                ls2 = p.val("Sc")
    
                def ln0_v3(v, ls2=ls2):
                    p.wait(v, "Sc", ls2)
                    p.wait(v, "V")
                    for g in range(2):
                        p.inc("V", v.scalar_tensor_tensor(
                            stats[:, NBC + g:NBC + g + 1], stats[:, MUC + g:MUC + g + 1],
                            -1.0, stats[:, RSTDC + g:RSTDC + g + 1], ALU.mult, ALU.mult))
                body("vector", ln0_v3)
                lv3 = p.val("V")
    
                def ln0_s3(sc, sb=sb, lv3=lv3):
                    p.wait(sc, "V", lv3)
                    p.wait(sc, "Sc")
                    for g in range(2):
                        p.inc("Sc", sc.activation(ln0tm[:, g * 1024:(g + 1) * 1024],
                                                  xbuf[sb][:, g * 1024:(g + 1) * 1024], AF.Identity,
                                                  bias=stats[:, NBC + g:NBC + g + 1],
                                                  scale=stats[:, RSTDC + g:RSTDC + g + 1]))
                body("scalar", ln0_s3)
                ln0_c = p.val("Sc")
    
                def bnd_v(v, s=s):
                    if s > 0:
                        ln0T_c = ln0T[:, :].rearrange("q (c w) -> q c w", w=257)
                        p.inc("V", v.tensor_copy(ln0T_c[:, :, 0:1], ln0T_c[:, :, 256:257]))
                body("vector", bnd_v)
                bnd_c = p.val("V")
    
            if astage >= 2:
                for g in range(2):
                    def trl(t, g=g, ln0_c=ln0_c, bnd_c=bnd_c):
                        p.wait(t, "Sc", ln0_c)
                        p.wait(t, "V", max(bnd_c, pbbf_free))
                        for c in range(NCH):
                            inst = t.transpose(PBbf[:, c * 128:(c + 1) * 128],
                                               ln0tm[:, g * 1024 + c * 128: g * 1024 + (c + 1) * 128],
                                               eyeb[:, :])
                        p.inc("T", inst)
                    body("tensor", trl)
                    trl_c = p.val("T")
    
                    def evl(v, g=g, trl_c=trl_c):
                        p.wait(v, "T", trl_c)
                        ln0T_c = ln0T[:, :].rearrange("q (c w) -> q c w", w=257)
                        src = PBbf[:, :].rearrange("q (c w) -> q c w", w=128)
                        p.inc("V", v.tensor_copy(ln0T_c[:, :, 1 + g * 128: 1 + (g + 1) * 128], src))
                    body("vector", evl)
                    pbbf_free = p.val("V")
    
            if astage >= 3:
                if dump_attn and s == 0:
                    def d1(sync):
                        p.wait(sync, "V")
                        p.wait(sync, "Sc")
                        p.inc("dC", sync.dma_start(dbg_ab[:, 8192:10240], ln0tm[:, :]), 16)
                        p.inc("dC", sync.dma_start(dbg_af[:, 0:64], stats[:, :]), 16)
                        p.inc("dC", sync.dma_start(dbg_ab[:, 0:2056], ln0T[:, :]), 16)
                    body("sync", d1)

                def xx_v(v):
                    ln0T_c = ln0T[:, :].rearrange("q (c w) -> q c w", w=257)
                    xx_c = xxb[:, :].rearrange("q (c w) -> q c w", w=256)
                    p.inc("V", v.tensor_tensor(xx_c, ln0T_c[:, :, 0:256], ln0T_c[:, :, 1:257],
                                               ALU.subtract))
                body("vector", xx_v)
                xx_c_ = p.val("V")
    
            if astage >= 4:
                mix_c = {}
                proj_c = {}
                evac_c = {}
                for mi, m in enumerate("rkvg"):
                    mb = mixb[mi % 2]
    
                    def mk_mix(v, m=m, mi=mi, mb=mb, xx_c_=xx_c_):
                        p.wait(v, "V", xx_c_)
                        if mi >= 2:
                            p.wait(v, "T", proj_c["rkvg"[mi - 2]])
                        ln0T_c = ln0T[:, :].rearrange("q (c w) -> q c w", w=257)
                        for c in range(NCH):
                            inst = v.scalar_tensor_tensor(
                                mb[:, c * 256:(c + 1) * 256], xxb[:, c * 256:(c + 1) * 256],
                                maa[:, mi * 8 + c: mi * 8 + c + 1], ln0T_c[:, c, 1:257],
                                ALU.mult, ALU.add)
                        p.inc("V", inst)
                    body("vector", mk_mix)
                    mix_c[m] = p.val("V")
    
                    for g in range(2):
                        step = mi * 2 + g
                        bank0 = PB[(step % 2) * 2]
                        bank1 = PB[(step % 2) * 2 + 1]
    
                        def mk_proj(t, m=m, mi=mi, g=g, mb=mb, step=step, bank0=bank0, bank1=bank1,
                                    prev_ao_sc=prev_ao_sc):
                            p.wait(t, "V", mix_c[m])
                            if step >= 2:
                                ec = evac_c[step - 2]
                                if ec[0] == "V":
                                    p.wait(t, "V", ec[1])
                                else:
                                    p.wait(t, "Sc", ec[1])
                            elif prev_ao_sc > 0:
                                p.wait(t, "Sc", prev_ao_sc)
                            for k in range(NCH):
                                lhs = mb[:, k * 256 + g * 128: k * 256 + (g + 1) * 128]
                                t.matmul(bank0[:, :], lhs, WTs[m][:, k * D: k * D + 512],
                                         start=(k == 0), stop=(k == NCH - 1))
                                inst = t.matmul(bank1[:, :], lhs, WTs[m][:, k * D + 512:(k + 1) * D],
                                                start=(k == 0), stop=(k == NCH - 1))
                            p.inc("T", inst)
                        body("tensor", mk_proj)
                        pj_c = p.val("T")
    
                        if m == "g":
                            def ev_g(sc, g=g, bank0=bank0, bank1=bank1, pj_c=pj_c):
                                p.wait(sc, "T", pj_c)
                                sc.activation(gtm[:, g * 1024: g * 1024 + 512], bank0[:, :], AF.Silu)
                                p.inc("Sc", sc.activation(gtm[:, g * 1024 + 512:(g + 1) * 1024],
                                                          bank1[:, :], AF.Silu))
                            body("scalar", ev_g)
                            evac_c[step] = ("Sc", p.val("Sc"))
                        else:
                            def ev_p(v, m=m, g=g, bank0=bank0, bank1=bank1, pj_c=pj_c):
                                p.wait(v, "T", pj_c)
                                dst = {"r": rtm, "k": ktok, "v": vtm}[m]
                                if m == "r":
                                    v.tensor_mul(dst[:, g * 1024:g * 1024 + 512], bank0[:, :],
                                                 deckR[:, 0:512])
                                    inst = v.tensor_mul(dst[:, g * 1024 + 512:(g + 1) * 1024],
                                                        bank1[:, :], deckR[:, 512:1024])
                                elif m == "k":
                                    v.tensor_mul(dst[:, g * 1024:g * 1024 + 512], bank0[:, :],
                                                 deckK[:, 0:512])
                                    inst = v.tensor_mul(dst[:, g * 1024 + 512:(g + 1) * 1024],
                                                        bank1[:, :], deckK[:, 512:1024])
                                else:
                                    v.tensor_copy(dst[:, g * 1024:g * 1024 + 512], bank0[:, :])
                                    inst = v.tensor_copy(dst[:, g * 1024 + 512:(g + 1) * 1024],
                                                         bank1[:, :])
                                p.inc("V", inst)
                            body("vector", ev_p)
                            evac_c[step] = ("V", p.val("V"))
                    proj_c[m] = p.val("T")
                rk_ev = evac_c[3][1]
                v_ev = evac_c[5][1]
                g_ev = evac_c[7][1]
    
            if dump_attn and s == 0:
                def d2(sync):
                    p.wait(sync, "V")
                    p.inc("dC", sync.dma_start(dbg_ab[:, 2560:4608], rtm[:, :]), 16)
                body("sync", d2)

            if astage >= 5:
                for src_t, dst_t in [(rtm, rfm), (ktok, kfm)]:
                    for g in range(2):
                        for hh in range(2):
                            def trk(t, src_t=src_t, g=g, hh=hh, rk_ev=rk_ev, pf=pbbf_free):
                                p.wait(t, "V", max(rk_ev, pf))
                                for j in range(8):
                                    h = hh * 8 + j
                                    inst = t.transpose(
                                        PBbf[0:64, j * 128:(j + 1) * 128],
                                        src_t[:, g * 1024 + h * 64: g * 1024 + (h + 1) * 64],
                                        eyeb[:, :])
                                p.inc("T", inst)
                            body("tensor", trk)
                            trk_c = p.val("T")

                            def evk(v, dst_t=dst_t, g=g, hh=hh, trk_c=trk_c):
                                p.wait(v, "T", trk_c)
                                dfm = dst_t[0:64, :].rearrange("q (h w) -> q h w", w=256)
                                srcp = PBbf[0:64, :].rearrange("q (j w) -> q j w", w=128)
                                p.inc("V", v.tensor_copy(
                                    dfm[:, hh * 8:(hh + 1) * 8, g * 128:(g + 1) * 128], srcp))
                            body("vector", evk)
                            pbbf_free = p.val("V")

            evrk = p.val("V")
    
            if astage >= 6:
                if dump_attn and s == 0:
                    def d3(sync):
                        p.wait(sync, "V")
                        p.inc("dC", sync.dma_start(dbg_ab[0:64, 4608:8192],
                                                   rfm[0:64, 0:3584]), 16)
                    body("sync", d3)

                for q in range(2):
                    amask_c = {}
                    ymm_c = {}
                    supd_half = {}
                    for hg in range(4):
                        def sc_a(t, q=q, hg=hg, evrk=evrk, v_ev=v_ev, la=last_amask):
                            p.wait(t, "V", evrk)
                            p.wait(t, "V", v_ev)
                            p.wait(t, "V", amask_c[hg - 1] if hg > 0 else la)
                            for j in range(4):
                                h = hg * 4 + j
                                cc = h * 256 + q * 128
                                inst = t.matmul(PA[:, j * 128:(j + 1) * 128],
                                                kfm[0:64, cc:cc + 128],
                                                rfm[0:64, cc:cc + 128],
                                                start=True, stop=True)
                            p.inc("T", inst)
                        body("tensor", sc_a)
                        a_c = p.val("T")

                        def sc_m(v, hg=hg, a_c=a_c):
                            p.wait(v, "T", a_c)
                            for j in range(4):
                                h = hg * 4 + j
                                inst = v.tensor_mul(aring[:, j * 128:(j + 1) * 128],
                                                    PA[:, j * 128:(j + 1) * 128],
                                                    wmask[:, h * 128:(h + 1) * 128])
                            p.inc("V", inst)
                        body("vector", sc_m)
                        amask_c[hg] = p.val("V")
                        last_amask = amask_c[hg]

                        def sc_y(t, q=q, hg=hg, g_ev=g_ev):
                            p.wait(t, "V", amask_c[hg])
                            if hg == 0:
                                p.wait(t, "Sc", g_ev)
                            for j in range(4):
                                h = hg * 4 + j
                                pg = (h % 2) * 64
                                cg = (h // 2) % 4
                                cc = h * 256 + q * 128
                                t.matmul(PY[pg:pg + 64, cg * 128:(cg + 1) * 128],
                                         sSb[0:64, h * 64:(h + 1) * 64],
                                         rfm[0:64, cc:cc + 128],
                                         start=True, stop=False, tile_position=(0, pg))
                                t.matmul(PY[pg:pg + 64, cg * 128:(cg + 1) * 128],
                                         vtm[:, q * 1024 + h * 64: q * 1024 + (h + 1) * 64],
                                         aring[:, j * 128:(j + 1) * 128],
                                         start=False, stop=True, tile_position=(0, pg))
                                inst = t.matmul(PT[0:64, (h % 8) * 64:(h % 8 + 1) * 64],
                                                ktok[:, q * 1024 + h * 64: q * 1024 + (h + 1) * 64],
                                                vtm[:, q * 1024 + h * 64: q * 1024 + (h + 1) * 64],
                                                start=True, stop=True)
                            p.inc("T", inst)
                        body("tensor", sc_y)
                        ymm_c[hg] = p.val("T")

                        if hg % 2 == 1:
                            half = hg // 2

                            def ev_y(v, hg=hg):
                                p.wait(v, "T", ymm_c[hg])
                                half_ = hg // 2
                                p.inc("V", v.tensor_copy(yfm[:, half_ * 512:(half_ + 1) * 512],
                                                         PY[:, :]))
                            body("vector", ev_y)

                            def s_upd(v, q=q, half=half, hg=hg):
                                p.wait(v, "T", ymm_c[hg])
                                sl = slice(half * 512, (half + 1) * 512)
                                v.tensor_add(sS[0:64, sl], sS[0:64, sl], PT[0:64, :])
                                wq_b = wQt[0:64, half * 8:(half + 1) * 8].unsqueeze(2)\
                                    .broadcast_to([64, 8, 64])
                                v.tensor_tensor(sS[0:64, sl].rearrange("q (c w) -> q c w", w=64),
                                                sS[0:64, sl].rearrange("q (c w) -> q c w", w=64),
                                                wq_b, ALU.mult)
                                p.inc("V", v.tensor_copy(sSb[0:64, sl], sS[0:64, sl]))
                            body("vector", s_upd)
                            supd_half[half] = p.val("V")
                    yfm_c = p.val("V")

                    def tr_y(t, q=q, yfm_c=yfm_c, g_ev=g_ev):
                        p.wait(t, "V", yfm_c)
                        p.wait(t, "Sc", g_ev)
                        for cg in range(8):
                            bank = PB[2] if cg < 4 else PB[3]
                            inst = t.transpose(bank[:, (cg % 4) * 128:(cg % 4 + 1) * 128],
                                               yfm[:, cg * 128:(cg + 1) * 128], eyef[:, :])
                            if cg in (3, 7):
                                p.inc("T", inst)
                    body("tensor", tr_y)
                    try_c = p.val("T")

                    def ev_ytm(v, try_c=try_c):
                        p.wait(v, "T", try_c - 1)
                        v.tensor_copy(ytm[:, 0:512], PB[2][:, :])
                        p.wait(v, "T", try_c)
                        p.inc("V", v.tensor_copy(ytm[:, 512:1024], PB[3][:, :]))
                    body("vector", ev_ytm)
                    evy_c = p.val("V")

                    if dump_attn and s == 0 and q == 0:
                        def d4(sync, evy_c=evy_c):
                            p.wait(sync, "V", evy_c)
                            p.inc("dC", sync.dma_start(dbg_af[:, 0:1024], ytm[:, :]), 16)
                            p.inc("dC", sync.dma_start(dbg_af[:, 1024:2048], yfm[:, :]), 16)
                        body("sync", d4)

                    def gn_s(sc, evy_c=evy_c):
                        p.wait(sc, "V", evy_c)
                        p.inc("Sc", sc.activation(sqdmp[:, :], ytm[:, :], AF.Square))
                    body("scalar", gn_s)
                    gsq_c = p.val("Sc")

                    def gn_v(v, gsq_c=gsq_c):
                        p.wait(v, "Sc", gsq_c)
                        yv = ytm[:, :].rearrange("q (h w) -> q h w", w=HN)
                        sq = sqdmp[:, :].rearrange("q (h w) -> q h w", w=HN)
                        v.reduce_sum(gstats[:, 0:16].unsqueeze(2), yv, AX.X)
                        p.flush(v, "V", v.reduce_sum(gstats[:, 16:32].unsqueeze(2), sq, AX.X))
                        v.tensor_scalar(gstats[:, 0:16], gstats[:, 0:16], 1.0 / HN, None, ALU.mult)
                        p.flush(v, "V", v.tensor_scalar(gstats[:, 16:32], gstats[:, 16:32],
                                        1.0 / HN, EPS_GN, ALU.mult, ALU.add))
                        p.flush(v, "V", v.tensor_mul(gstats[:, 32:48], gstats[:, 0:16],
                                                     gstats[:, 0:16]))
                        p.flush(v, "V", v.tensor_sub(gstats[:, 16:32], gstats[:, 16:32],
                                                     gstats[:, 32:48]))
                        p.inc("V", v.reciprocal(gstats[:, 16:32], gstats[:, 16:32]))
                    body("vector", gn_v)
                    gnv_c = p.val("V")

                    def gn_s2(sc, gnv_c=gnv_c):
                        p.wait(sc, "V", gnv_c)
                        p.inc("Sc", sc.activation(gstats[:, 16:32], gstats[:, 16:32], AF.Sqrt))
                    body("scalar", gn_s2)
                    gns2_c = p.val("Sc")

                    def gn_ap(v, q=q, gns2_c=gns2_c):
                        p.wait(v, "Sc", gns2_c)
                        p.wait(v, "V")
                        if dump_attn and s == 0 and q == 0:
                            p.wait(v, "dC")
                        yv = ytm[:, :].rearrange("q (h w) -> q h w", w=HN)
                        mu_b = gstats[:, 0:16].unsqueeze(2).broadcast_to([128, 16, HN])
                        rs_b = gstats[:, 16:32].unsqueeze(2).broadcast_to([128, 16, HN])
                        v.tensor_tensor(yv, yv, mu_b, ALU.subtract)
                        v.tensor_tensor(yv, yv, rs_b, ALU.mult)
                        v.tensor_mul(ytm[:, :], ytm[:, :], lngb[:, :])
                        v.tensor_add(ytm[:, :], ytm[:, :], lnbb[:, :])
                        p.inc("V", v.tensor_mul(ytotm[:, :], ytm[:, :],
                                                gtm[:, q * 1024:(q + 1) * 1024]))
                    body("vector", gn_ap)
                    gna_c = p.val("V")

                    def tr_yt(t, gna_c=gna_c, pf=pbbf_free):
                        p.wait(t, "V", max(gna_c, pf))
                        for c in range(NCH):
                            inst = t.transpose(PBbf[:, c * 128:(c + 1) * 128],
                                               ytotm[:, c * 128:(c + 1) * 128], eyeb[:, :])
                        p.inc("T", inst)
                    body("tensor", tr_yt)
                    tryt_c = p.val("T")

                    def ev_yt(v, q=q, tryt_c=tryt_c):
                        p.wait(v, "T", tryt_c)
                        ytT = ytotT[:, :].rearrange("q (c w) -> q c w", w=256)
                        srcp = PBbf[:, :].rearrange("q (c w) -> q c w", w=128)
                        p.inc("V", v.tensor_copy(ytT[:, :, q * 128:(q + 1) * 128], srcp))
                    body("vector", ev_yt)
                    pbbf_free = p.val("V")
                evyt = p.val("V")

                for q in range(2):
                    def oproj(t, q=q, evyt=evyt):
                        p.wait(t, "V", evyt)
                        if q == 0:
                            ec = evac_c[6]
                            p.wait(t, "Sc", ec[1])
                        else:
                            p.wait(t, "Sc", evac_c[7][1])
                        for k in range(NCH):
                            lhs = ytotT[:, k * 256 + q * 128: k * 256 + (q + 1) * 128]
                            t.matmul(PB[q * 2][:, :], lhs, WTs["o"][:, k * D:k * D + 512],
                                     start=(k == 0), stop=(k == NCH - 1))
                            inst = t.matmul(PB[q * 2 + 1][:, :], lhs,
                                            WTs["o"][:, k * D + 512:(k + 1) * D],
                                            start=(k == 0), stop=(k == NCH - 1))
                        p.inc("T", inst)
                    body("tensor", oproj)
                    op_c = p.val("T")
    
                    def ao_ev(sc, q=q, op_c=op_c):
                        p.wait(sc, "T", op_c)
                        sc.copy(aot[:, 0:512], PB[q * 2][:, :])
                        sc.copy(aot[:, 512:1024], PB[q * 2 + 1][:, :])
                        p.inc("Sc", sc.activation(sqdmp[:, :], aot[:, :], AF.Square,
                                                  accum_out=stats[:, SQC:SQC + 1]))
                    body("scalar", ao_ev)
                    aoe_c = p.val("Sc")
    
                    def ao_v(v, aoe_c=aoe_c):
                        p.wait(v, "Sc", aoe_c)
                        p.flush(v, "V", v.reduce_sum(stats[:, SUMC:SUMC + 1].unsqueeze(2),
                                     aot[:, :].unsqueeze(1), AX.X))
                        v.tensor_scalar(stats[:, MUC:MUC + 1], stats[:, SUMC:SUMC + 1], 1.0 / D,
                                        None, ALU.mult)
                        p.flush(v, "V", v.tensor_scalar(stats[:, E2C:E2C + 1],
                                        stats[:, SQC:SQC + 1], 1.0 / D,
                                        EPS_LN, ALU.mult, ALU.add))
                        p.flush(v, "V", v.tensor_mul(stats[:, M2C:M2C + 1], stats[:, MUC:MUC + 1],
                                     stats[:, MUC:MUC + 1]))
                        p.flush(v, "V", v.tensor_sub(stats[:, VARC:VARC + 1], stats[:, E2C:E2C + 1],
                                     stats[:, M2C:M2C + 1]))
                        p.inc("V", v.reciprocal(stats[:, VARC:VARC + 1], stats[:, VARC:VARC + 1]))
                    body("vector", ao_v)
                    aov_c = p.val("V")
    
                    def ao_s2(sc, aov_c=aov_c):
                        p.wait(sc, "V", aov_c)
                        p.inc("Sc", sc.activation(stats[:, RSTDC:RSTDC + 1], stats[:, VARC:VARC + 1],
                                                  AF.Sqrt))
                    body("scalar", ao_s2)
                    aos2_c = p.val("Sc")
    
                    def ao_v2(v, aos2_c=aos2_c):
                        p.wait(v, "Sc", aos2_c)
                        p.wait(v, "V")
                        p.inc("V", v.scalar_tensor_tensor(
                            stats[:, NBC:NBC + 1], stats[:, MUC:MUC + 1], -1.0,
                            stats[:, RSTDC:RSTDC + 1], ALU.mult, ALU.mult))
                    body("vector", ao_v2)
                    aov2_c = p.val("V")
    
                    def ao_s3(sc, q=q, sb=sb, aov2_c=aov2_c):
                        p.wait(sc, "V", aov2_c)
                        p.wait(sc, "Sc")
                        p.inc("Sc", sc.activation(xbuf[sb][:, q * 1024:(q + 1) * 1024],
                                                  xbuf[sb][:, q * 1024:(q + 1) * 1024], AF.Identity,
                                                  bias=stats[:, NBC:NBC + 1]))
                    body("scalar", ao_s3)
                    aos3_c = p.val("Sc")
    
                    def ao_v3(v, q=q, sb=sb, aos3_c=aos3_c):
                        p.wait(v, "Sc", aos3_c)
                        p.inc("V", v.scalar_tensor_tensor(
                            xbuf[sb][:, q * 1024:(q + 1) * 1024], aot[:, :],
                            stats[:, RSTDC:RSTDC + 1], xbuf[sb][:, q * 1024:(q + 1) * 1024],
                            ALU.mult, ALU.add))
                    body("vector", ao_v3)
                    x1_c = p.val("V")
    
                    def x1_st(sc, q=q, s=s, sb=sb, x1_c=x1_c):
                        p.wait(sc, "V", x1_c)
                        p.inc(f"dSx{sb}", sc.dma_start(
                            x1_scr[s * 256 + q * 128: s * 256 + (q + 1) * 128, :],
                            xbuf[sb][:, q * 1024:(q + 1) * 1024]), 16)
                    body("scalar", x1_st)
                prev_ao_sc = p.val("Sc")
                x1_store_cnt[s] = p.val(f"dSx{sb}")

        def quant_weight(wsrc, wqdst, nrow, ncolp, mcol):
            pieces = [(r, c) for r in range(nrow) for c in range(ncolp)]

            def q_init(v):
                p.inc("V", v.memset(stats[:, 16:17], 0.0))
            body("vector", q_init)

            abs_cnts = []
            for i, (r, c) in enumerate(pieces):
                def q_ld(sync, r=r, c=c, i=i):
                    p.wait(sync, "V", abs_cnts[i - 2] if i >= 2 else p.val("V"))
                    p.inc("dW", sync.dma_start(
                        wstg[i % 2][:, :],
                        wsrc[r * 128:(r + 1) * 128, c * 1024:(c + 1) * 1024]), 16)
                body("sync", q_ld)

                def q_abs(v, i=i):
                    p.wait(v, "dW")
                    p.flush(v, "V", v.reduce_sum(stats[:, 17:18].unsqueeze(2),
                                 wstg[i % 2][:, :].unsqueeze(1),
                                 AX.X, apply_absolute_value=True))
                    p.inc("V", v.tensor_add(stats[:, 16:17], stats[:, 16:17], stats[:, 17:18]))
                body("vector", q_abs)
                abs_cnts.append(p.val("V"))

            def q_sum(t):
                p.wait(t, "V")
                p.inc("T", t.matmul(PA[0:1, 256:257], stats[:, 16:17], onecol[:, 2:3],
                                    start=True, stop=True))
            body("tensor", q_sum)

            nelem = nrow * 128 * ncolp * 1024

            def q_m(v):
                p.wait(v, "T")
                p.inc("V", v.tensor_scalar(smalls[0:1, 62:63], PA[0:1, 256:257], 1.0 / nelem,
                                           1e-5, ALU.mult, ALU.max))
            body("vector", q_m)

            def q_bc(t):
                p.wait(t, "V")
                p.inc("T", t.matmul(PA[:, 0:1], ones1[:, :], smalls[0:1, 62:63],
                                    start=True, stop=True))
            body("tensor", q_bc)

            def q_f(v):
                p.wait(v, "T")
                v.tensor_copy(smalls[:, mcol:mcol + 1], PA[:, 0:1])
                p.inc("V", v.reciprocal(smalls[:, mcol + 1:mcol + 2], PA[:, 0:1]))
            body("vector", q_f)

            rnd_cnts = []
            st_cnts = {0: [], 1: []}
            for i, (r, c) in enumerate(pieces):
                def q_ld2(sync, r=r, c=c, i=i):
                    p.wait(sync, "V", rnd_cnts[i - 2] if i >= 2 else p.val("V"))
                    p.inc("dW", sync.dma_start(
                        wstg[i % 2][:, :],
                        wsrc[r * 128:(r + 1) * 128, c * 1024:(c + 1) * 1024]), 16)
                body("sync", q_ld2)

                def q_rnd(v, i=i):
                    p.wait(v, "dW")
                    p.wait(v, f"dQ{i % 2}")
                    v.tensor_scalar(wstg[i % 2][:, :], wstg[i % 2][:, :],
                                    smalls[:, mcol + 1:mcol + 2], 1.49, ALU.mult, ALU.min)
                    v.tensor_scalar(wstg[i % 2][:, :], wstg[i % 2][:, :], -1.49, None, ALU.max)
                    v.tensor_scalar(wstg[i % 2][:, :], wstg[i % 2][:, :],
                                    smalls[:, 42:43], None, ALU.add)
                    v.tensor_scalar(wstg[i % 2][:, :], wstg[i % 2][:, :],
                                    smalls[:, 42:43], None, ALU.subtract)
                    p.inc("V", v.tensor_copy(wqbf[i % 2][:, :], wstg[i % 2][:, :]))
                body("vector", q_rnd)
                rnd_cnts.append(p.val("V"))

                def q_st(sc, r=r, c=c, i=i):
                    p.wait(sc, "V", rnd_cnts[i])
                    p.inc(f"dQ{i % 2}", sc.dma_start(
                        wqdst[r * 128:(r + 1) * 128, c * 1024:(c + 1) * 1024],
                        wqbf[i % 2][:, :]), 16)
                body("scalar", q_st)

        quant_weight(w1T, w1q_scr, NCH, 4, 60)
        quant_weight(w2T, w2q_scr, NHID, 1, 58)

        def bar_sy(sync):
            for sn in ["dC", "dX", "dW"]:
                p.wait(sync, sn)
        body("sync", bar_sy)

        def bar_v(v):
            p.inc("bar", v.memset(onecol[:, 3:4], 0.0))
        body("vector", bar_v)

        def bar_s(sc):
            for sn in ["dSx0", "dSx1", "dQ0", "dQ1"]:
                p.wait(sc, sn)
            p.inc("bar", sc.copy(onecol[:, 4:5], onecol[:, 3:4]))
        body("scalar", bar_s)

        def bar_t(t):
            p.wait(t, "V")
            p.wait(t, "Sc")
            p.wait(t, "G")
            p.inc("bar", t.matmul(PT[0:1, 0:1], stats[0:1, 0:1], onecol[0:1, 2:3],
                                  start=True, stop=True))
        body("tensor", bar_t)

        def bar_g(g):
            p.inc("bar", g.memset(onecol[:, 5:6], 0.0))
        body("gpsimd", bar_g)
        barv = p.val("bar")

        att.close()
        patt.close()

        if stub_ffn:
            ffn = contextlib.ExitStack(); stack.enter_context(ffn)
            pffn = contextlib.ExitStack(); stack.enter_context(pffn)
            cpbuf = [alloc(ffn, f"cpbuf{i}", [128, 1024]) for i in range(2)]

            def fs_bar(sync):
                p.wait(sync, "bar", barv)
            body("sync", fs_bar)

            def fs_bar2(sc):
                p.wait(sc, "bar", barv)
            body("scalar", fs_bar2)

            st_cnt = []
            for i in range(T // 128):
                def fs_ld(sync, i=i):
                    if i >= 2:
                        p.wait(sync, f"dO{i % 2}", st_cnt[i - 2])
                    p.inc("dF", sync.dma_start(cpbuf[i % 2][:, :],
                                               x1_scr[i * 128:(i + 1) * 128, :]), 16)
                body("sync", fs_ld)

                def fs_st(sc, i=i):
                    p.wait(sc, "dF")
                    p.inc(f"dO{i % 2}", sc.dma_start(out[i * 128:(i + 1) * 128, :],
                                                     cpbuf[i % 2][:, :]), 16)
                body("scalar", fs_st)
                st_cnt.append(p.val(f"dO{i % 2}"))

            def fs_fin(sc):
                p.wait(sc, "dO0")
                p.wait(sc, "dO1")
            body("scalar", fs_fin)
        else:
            NT = T // 256
            ffn = contextlib.ExitStack(); stack.enter_context(ffn)
            pffn = contextlib.ExitStack(); stack.enter_context(pffn)
            w1qs = alloc(ffn, "w1qs", [128, NCH * DH], BF16)
            w2p = [alloc(ffn, f"w2p{i}", [128, 4 * D], BF16) for i in range(2)]
            x1a = [alloc(ffn, f"x1a{i}", [128, 2048]) for i in range(2)]
            x1s = [alloc(ffn, f"x1s{i}", [128, 1024]) for i in range(2)]
            outb = [alloc(ffn, f"outb{i}", [128, 1024]) for i in range(2)]
            xqb = alloc(ffn, "xqb", [128, 2048], BF16)
            xqf = alloc(ffn, "xqf", [128, 1024])
            xqT = [alloc(ffn, f"xqT{i}", [128, 2048], BF16) for i in range(2)]
            hsil = alloc(ffn, "hsil", [128, 32 * 260])
            hbuf = [alloc(ffn, f"hbuf{i}", [128, 260]) for i in range(2)]
            cv1 = [alloc(ffn, f"cv1_{i}", [128, 260]) for i in range(2)]
            gtmp = [alloc(ffn, f"gtmp{i}", [128, 260]) for i in range(2)]
            mmap2 = alloc(ffn, "mmap2", [128, 520])
            carries = alloc(ffn, "carries", [128, 64])
            mmap = alloc(ffn, "mmap", [128, 260])
            hq = alloc(ffn, "hq", [128, 32 * 260], BF16)
            c1s = [alloc(ffn, f"c1s{i}", [128, 256]) for i in range(2)]
            rowsb = alloc(ffn, "rowsb", [1, 260])
            ao2 = alloc(ffn, "ao2", [128, 1024])
            fdump = alloc(ffn, "fdump", [128, 1024], BF16)

            Q0 = palloc(pffn, "Q0", [128, 512])
            Q1 = palloc(pffn, "Q1", [128, 512])
            C0 = palloc(pffn, "C0", [128, 1024])
            C1 = palloc(pffn, "C1", [128, 1024])
            Cbf = palloc(pffn, "Cbf", [128, 1024], BF16)
            QB = palloc(pffn, "QB", [128, 512])

            def tile_geom(t):
                c_lo = max(256 * t - 1, 0)
                c_hi = 256 * t + 255 if t < NT - 1 else T
                Wt = c_hi - c_lo
                if t == 0:
                    groups = [(0, 128), (128, 127)]
                elif t < NT - 1:
                    groups = [(0, 128), (128, 128)]
                else:
                    groups = [(0, 128), (128, 128), (256, 1)]
                return c_lo, c_hi, Wt, groups

            for en in ["sync", "vector", "scalar", "tensor", "gpsimd"]:
                def fb(e, en=en):
                    p.wait(e, "bar", barv)
                body(en, fb)

            def f_ld0(sync):
                for k in range(NCH):
                    p.inc("dF", sync.dma_start(w1qs[:, k * DH:(k + 1) * DH],
                                               w1q_scr[k * 128:(k + 1) * 128, :]), 16)
                for j in range(2):
                    for c in range(4):
                        p.inc(f"dF{j}", sync.dma_start(
                            w2p[j][:, c * D:(c + 1) * D],
                            w2q_scr[(j * 4 + c) * 128:(j * 4 + c + 1) * 128, :]), 16)
            body("sync", f_ld0)
            w1qs_ld = p.val("dF")

            def f_init(v):
                v.memset(carries[:, :], 0.0)
                hs_c = hsil[:, :].rearrange("q (c w) -> q c w", w=260)
                v.memset(hs_c[:, :, 258:260], 0.0)
                v.memset(hbuf[0][:, 258:260], 0.0)
                p.inc("V", v.memset(hbuf[1][:, 258:260], 0.0))
            body("vector", f_init)

            ln2_done = {}
            hq_done = [0] * NT
            mm2_done = [0] * (NT + 1)
            mm1_done = [0] * NT
            out_idx = [0]
            x1a_used = [0] * NT
            x1s_st = {}

            def emit_ln2(t):
                tb = t % 2

                def l_ld(sync, t=t, tb=tb):
                    if t >= 2:
                        p.wait(sync, "V", x1a_used[t - 2])
                    for g in range(2):
                        p.inc(f"dL{tb}", sync.dma_start(
                            x1a[tb][:, g * 1024:(g + 1) * 1024],
                            x1_scr[t * 256 + g * 128: t * 256 + (g + 1) * 128, :]), 16)
                body("sync", l_ld)
                lld_c = p.val(f"dL{tb}")

                for g in range(2):
                    def l_v1(v, t=t, tb=tb, g=g, lld_c=lld_c):
                        p.wait(v, f"dL{tb}", lld_c)
                        xg = x1a[tb][:, g * 1024:(g + 1) * 1024]
                        v.reduce_sum(stats[:, 0:1].unsqueeze(2), xg.unsqueeze(1), AX.X)
                        v.reduce_max(stats[:, 1:2].unsqueeze(2), xg.unsqueeze(1), AX.X)
                        p.inc("V", v.tensor_reduce(stats[:, 2:3].unsqueeze(2), xg.unsqueeze(1),
                                                   AX.X, ALU.min))
                    body("vector", l_v1)
                    lv1 = p.val("V")

                    def l_s1(sc, t=t, tb=tb, g=g, lv1=lv1):
                        p.wait(sc, "V", lv1)
                        p.inc("Sc", sc.activation(fdump[:, :], x1a[tb][:, g * 1024:(g + 1) * 1024],
                                                  AF.Square, accum_out=stats[:, 3:4]))
                    body("scalar", l_s1)
                    ls1 = p.val("Sc")

                    def l_v2(v, ls1=ls1):
                        p.wait(v, "Sc", ls1)
                        p.wait(v, "V")
                        v.tensor_scalar(stats[:, 4:5], stats[:, 0:1], 1.0 / D, None, ALU.mult)
                        p.flush(v, "V", v.tensor_scalar(stats[:, 5:6], stats[:, 3:4], 1.0 / D,
                                        EPS_LN, ALU.mult, ALU.add))
                        p.flush(v, "V", v.tensor_mul(stats[:, 6:7], stats[:, 4:5], stats[:, 4:5]))
                        p.flush(v, "V", v.tensor_sub(stats[:, 5:6], stats[:, 5:6], stats[:, 6:7]))
                        p.inc("V", v.reciprocal(stats[:, 5:6], stats[:, 5:6]))
                    body("vector", l_v2)
                    lv2 = p.val("V")

                    def l_s2(sc, lv2=lv2):
                        p.wait(sc, "V", lv2)
                        p.inc("Sc", sc.activation(stats[:, 5:6], stats[:, 5:6], AF.Sqrt))
                    body("scalar", l_s2)
                    ls2 = p.val("Sc")

                    def l_v3(v, g=g, ls2=ls2):
                        p.wait(v, "Sc", ls2)
                        p.wait(v, "V")
                        v.tensor_sub(stats[:, 7:8], stats[:, 1:2], stats[:, 4:5])
                        p.flush(v, "V", v.tensor_sub(stats[:, 8:9], stats[:, 4:5],
                                                     stats[:, 2:3]))
                        p.flush(v, "V", v.tensor_max(stats[:, 7:8], stats[:, 7:8],
                                                     stats[:, 8:9]))
                        p.flush(v, "V", v.scalar_tensor_tensor(stats[:, 7:8], stats[:, 7:8],
                                        1.0, stats[:, 5:6], ALU.mult, ALU.mult))
                        p.flush(v, "V", v.tensor_scalar(stats[:, 7:8], stats[:, 7:8], 1e-5,
                                        None, ALU.max))
                        p.flush(v, "V", v.reciprocal(stats[:, 9:10], stats[:, 7:8]))
                        p.flush(v, "V", v.tensor_scalar(stats[:, 9:10], stats[:, 9:10], 127.0,
                                        None, ALU.mult))
                        p.flush(v, "V", v.tensor_mul(stats[:, 10:11], stats[:, 5:6],
                                        stats[:, 9:10]))
                        v.scalar_tensor_tensor(stats[:, 11:12], stats[:, 4:5], -1.0,
                                               stats[:, 10:11], ALU.mult, ALU.mult)
                        p.inc("V", v.scalar_tensor_tensor(stats[:, 12 + g:13 + g],
                                        stats[:, 7:8], 1.0 / 127.0, smalls[:, 60:61],
                                        ALU.mult, ALU.mult))
                    body("vector", l_v3)
                    lv3 = p.val("V")

                    def l_s3(sc, t=t, tb=tb, g=g, lv3=lv3):
                        p.wait(sc, "V", lv3)
                        p.inc("Sc", sc.activation(xqf[:, :], x1a[tb][:, g * 1024:(g + 1) * 1024],
                                                  AF.Identity, scale=stats[:, 10:11],
                                                  bias=stats[:, 11:12]))
                    body("scalar", l_s3)
                    ls3 = p.val("Sc")

                    def l_v4(v, g=g, ls3=ls3):
                        p.wait(v, "Sc", ls3)
                        v.tensor_scalar(xqf[:, :], xqf[:, :], smalls[:, 42:43], None, ALU.add)
                        v.tensor_scalar(xqf[:, :], xqf[:, :], smalls[:, 42:43], None, ALU.subtract)
                        p.inc("V", v.tensor_copy(xqb[:, g * 1024:(g + 1) * 1024], xqf[:, :]))
                    body("vector", l_v4)
                x1a_used[t] = p.val("V")

            def emit_ln2_tail(t):
                tb = t % 2
                def l_t1(t_, t=t):
                    p.wait(t_, "V", x1a_used[t])
                    t_.transpose(QB[0:1, 0:128], stats[:, 12:13], eyef[:, :])
                    p.inc("T", t_.transpose(QB[0:1, 128:256], stats[:, 13:14], eyef[:, :]))
                body("tensor", l_t1)
                lt1 = p.val("T")

                def l_v5(v, lt1=lt1):
                    p.wait(v, "T", lt1)
                    v.tensor_copy(rowsb[0:1, 0:128], QB[0:1, 0:128])
                    p.inc("V", v.tensor_copy(rowsb[0:1, 128:256], QB[0:1, 128:256]))
                body("vector", l_v5)
                lv5 = p.val("V")

                def l_t2(t_, lv5=lv5):
                    p.wait(t_, "V", lv5)
                    p.inc("T", t_.matmul(QB[:, 0:256], ones1[:, :], rowsb[0:1, 0:256],
                                         start=True, stop=True))
                body("tensor", l_t2)
                lt2 = p.val("T")

                def l_v6(v, lt2=lt2):
                    p.wait(v, "T", lt2)
                    p.inc("V", v.tensor_copy(c1s[t % 2][:, :], QB[:, 0:256]))
                body("vector", l_v6)
                ln2_done[t] = p.val("V")

                for g in range(2):
                    def l_t3(t_, g=g, t=t):
                        p.wait(t_, "V", ln2_done[t] if g == 0 else p.val("V"))
                        p.wait(t_, "V", pbbf_f[0])
                        for c in range(NCH):
                            inst = t_.transpose(Cbf[:, c * 128:(c + 1) * 128],
                                                xqb[:, g * 1024 + c * 128: g * 1024 + (c + 1) * 128],
                                                eyeb[:, :])
                        p.inc("T", inst)
                    body("tensor", l_t3)
                    lt3 = p.val("T")

                    def l_v7(v, g=g, lt3=lt3):
                        p.wait(v, "T", lt3)
                        xT = xqT[t % 2][:, :].rearrange("q (c w) -> q c w", w=256)
                        srcp = Cbf[:, :].rearrange("q (c w) -> q c w", w=128)
                        p.inc("V", v.tensor_copy(xT[:, :, g * 128:(g + 1) * 128], srcp))
                    body("vector", l_v7)
                    pbbf_f[0] = p.val("V")
                ln2_xqT[t] = p.val("V")

            pbbf_f = [p.val("V")]
            mm1ev_last = [0]
            conv_done = [0] * NT
            carry_sc = [0] * NT

            def emit_mm1(t):
                c_lo, c_hi, Wt, groups = tile_geom(t)
                off = 1 if t == 0 else 0
                evac_cnt = {}
                for po in range(NHID // 2):
                    o0, o1 = 2 * po, 2 * po + 1
                    bank = Q0 if po % 2 == 0 else Q1

                    def m1(t_, t=t, po=po, o0=o0, o1=o1, bank=bank):
                        if po == 0:
                            p.wait(t_, "V", ln2_xqT[t])
                            if t == 0:
                                p.wait(t_, "dF", w1qs_ld)
                        if po >= 2:
                            p.wait(t_, "V", evac_cnt[po - 2])
                            p.wait(t_, "Sc", carry_cnt[2 * (po - 2) + 1])
                        elif t > 0:
                            p.wait(t_, "V", mm1ev_last[0])
                            p.wait(t_, "Sc", carry_cnt[2 * (po + 14) + 1])
                        for oo, half in ((o0, 0), (o1, 256)):
                            for k in range(NCH):
                                inst = t_.matmul(bank[:, half:half + 256],
                                                 w1qs[:, k * DH + oo * 128: k * DH + (oo + 1) * 128],
                                                 xqT[t % 2][:, k * 256:(k + 1) * 256],
                                                 start=(k == 0), stop=(k == NCH - 1))
                        p.inc("T", inst)
                    body("tensor", m1)
                    m1c = p.val("T")

                    for oo, half in ((o0, 0), (o1, 256)):
                        def m1e(v, t=t, oo=oo, half=half, bank=bank, m1c=m1c):
                            p.wait(v, "T", m1c)
                            p.wait(v, "Sc", cts1_cnt[oo - 2] if oo >= 2 else cts1_cnt[oo + 30])
                            v.tensor_tensor(hbuf[oo % 2][:, 2:258],
                                            bank[:, half:half + 256], c1s[t % 2][:, :], ALU.mult)
                            p.inc("V", v.tensor_copy(hbuf[oo % 2][:, 0:2],
                                                     carries[:, oo * 2:oo * 2 + 2]))
                        body("vector", m1e)
                        evh = p.val("V")

                        def m1c_(sc, oo=oo, evh=evh):
                            p.wait(sc, "V", evh)
                            p.inc("Sc", sc.copy(carries[:, oo * 2:oo * 2 + 2],
                                                hbuf[oo % 2][:, 256:258]))
                        body("scalar", m1c_)
                        carry_cnt[oo] = p.val("Sc")

                        def cts1(sc, t=t, oo=oo, off=off, Wt=Wt):
                            p.wait(sc, "V", stt2_cnt[oo - 2] if oo >= 2 else stt2_cnt[oo + 30])
                            p.inc("Sc", sc.activation(cv1[oo % 2][:, 0:Wt],
                                                      hbuf[oo % 2][:, off + 2:off + 2 + Wt],
                                                      AF.Identity,
                                                      scale=convw_s[:, 64 + oo:65 + oo],
                                                      bias=convb_s[:, oo:oo + 1]))
                        body("scalar", cts1)
                        cts1c = p.val("Sc")
                        cts1_cnt[oo] = cts1c

                        def cstt1(v, oo=oo, off=off, Wt=Wt, cts1c=cts1c):
                            p.wait(v, "Sc", cts1c)
                            v.scalar_tensor_tensor(cv1[oo % 2][:, 0:Wt],
                                                   hbuf[oo % 2][:, off + 1:off + 1 + Wt],
                                                   convw_s[:, 32 + oo:33 + oo],
                                                   cv1[oo % 2][:, 0:Wt], ALU.mult, ALU.add)
                            p.inc("V", v.scalar_tensor_tensor(hsil[:, oo * 260:oo * 260 + Wt],
                                                              hbuf[oo % 2][:, off:off + Wt],
                                                              convw_s[:, oo:oo + 1],
                                                              cv1[oo % 2][:, 0:Wt],
                                                              ALU.mult, ALU.add))
                        body("vector", cstt1)
                        stt2_cnt[oo] = p.val("V")
                    evac_cnt[po] = p.val("V")
                mm1ev_last[0] = p.val("V")
                mm1_done[t] = p.val("T")
                conv_done[t] = p.val("V")


            carry_cnt = [0] * NHID
            stt2_cnt = [0] * NHID
            cts1_cnt = [0] * NHID
            ln2_xqT = [0] * (NT + 1)

            def emit_carry_fill(t):
                pass

            def emit_quant2(t):
                c_lo, c_hi, Wt, groups = tile_geom(t)

                def qs_silu(sc, t=t, Wt=Wt):
                    p.wait(sc, "V", conv_done[t])
                    hs_c = hsil[:, :].rearrange("q (c w) -> q c w", w=260)[:, :, 0:Wt]
                    p.inc("Sc", sc.activation(hs_c, hs_c, AF.Silu))
                body("scalar", qs_silu)
                silu_c = p.val("Sc")
                if qstage < 2:
                    return

                if dump_hsil and t == NT - 1:
                    def qdmp(sc, silu_c=silu_c, Wt=Wt):
                        p.wait(sc, "Sc", silu_c)
                        dv = dbg_h.rearrange("q (c w) -> q c w", w=260)[:, :, 0:Wt]
                        hv = hsil[:, :].rearrange("q (c w) -> q c w", w=260)[:, :, 0:Wt]
                        p.inc("dF", sc.dma_start(dv, hv), 16)
                    body("scalar", qdmp)

                def qv_m0(v, silu_c=silu_c, Wt=Wt):
                    p.wait(v, "Sc", silu_c)
                    p.inc("V", v.memset(mmap[:, :], 0.0))
                body("vector", qv_m0)

                sq_cnt = {}
                mx_cnt = {}
                for o in range(NHID):
                    def qv_mx(v, o=o, Wt=Wt, silu_c=silu_c):
                        if o == 0:
                            p.wait(v, "Sc", silu_c)
                        hc = hsil[:, o * 260:o * 260 + Wt]
                        v.tensor_tensor(mmap2[:, (o % 2) * 260:(o % 2) * 260 + Wt],
                                        hc, hc, ALU.mult)
                        p.inc("V", v.tensor_tensor(mmap[:, 0:Wt], mmap[:, 0:Wt],
                                                   mmap2[:, (o % 2) * 260:(o % 2) * 260 + Wt],
                                                   ALU.max))
                    body("vector", qv_mx)
                    mx_cnt[o] = p.val("V")
                am_c = p.val("V")
                if qstage < 3:
                    return

                npieces = (Wt + 127) // 128
                qvr_cnt = {}
                qb_read = [am_c]
                for pi in range(npieces):
                    wp = min(128, Wt - pi * 128)
                    rwp = min(128, 260 - pi * 128)

                    def qt_m(t_, pi=pi, rwp=rwp):
                        p.wait(t_, "V", qb_read[0])
                        p.inc("T", t_.transpose(QB[0:rwp, 256 + (pi % 2) * 128:256 + (pi % 2) * 128 + 128],
                                                mmap[:, pi * 128:pi * 128 + rwp], eyef[:, :]))
                    body("tensor", qt_m)
                    qtm = p.val("T")

                    def qv_r0(v, pi=pi, rwp=rwp, qtm=qtm):
                        p.wait(v, "T", qtm)
                        bounce = mmap2[:, (pi % 2) * 260:(pi % 2) * 260 + 128]
                        p.flush(v, "V", v.tensor_copy(
                            bounce[0:rwp, :],
                            QB[0:rwp, 256 + (pi % 2) * 128:256 + (pi % 2) * 128 + 128]))
                        p.inc("V", v.reduce_max(stats[0:rwp, 16 + pi:17 + pi].unsqueeze(2),
                                     bounce[0:rwp, :].unsqueeze(1), AX.X))
                    body("vector", qv_r0)
                    qv0 = p.val("V")
                    qb_read[0] = qv0
                    if qstage == 11:
                        continue

                    def qs_r(sc, pi=pi, wp=rwp, qv0=qv0):
                        p.wait(sc, "V", qv0)
                        p.inc("Sc", sc.activation(stats[0:wp, 16 + pi:17 + pi],
                                                  stats[0:wp, 16 + pi:17 + pi], AF.Sqrt))
                    body("scalar", qs_r)
                    qsr = p.val("Sc")
                    if qstage == 12:
                        continue

                    def qv_r(v, pi=pi, wp=rwp, t=t, qsr=qsr):
                        p.wait(v, "Sc", qsr)
                        p.wait(v, "V")
                        p.flush(v, "V", v.tensor_scalar(stats[0:wp, 16 + pi:17 + pi],
                                        stats[0:wp, 16 + pi:17 + pi],
                                        1e-5, None, ALU.max))
                        p.flush(v, "V", v.reciprocal(stats[0:wp, 20 + pi:21 + pi],
                                        stats[0:wp, 16 + pi:17 + pi]))
                        v.tensor_scalar(stats[0:wp, 20 + pi:21 + pi], stats[0:wp, 20 + pi:21 + pi],
                                        127.0, None, ALU.mult)
                        cc2 = 48 + (t % 2) * 4 + pi
                        p.inc("V", v.scalar_tensor_tensor(stats[0:wp, cc2:cc2 + 1],
                                        stats[0:wp, 16 + pi:17 + pi], 1.0 / 127.0,
                                        smalls[:, 58:59][0:wp, :], ALU.mult, ALU.mult))
                    body("vector", qv_r)
                    qvr_cnt[pi] = p.val("V")
                s2_c = p.val("V")
                if qstage < 4 or qstage in (10, 11, 12, 13, 21, 22, 23, 24, 25, 26):
                    return

                qrr_cnt = {}
                for pi in range(npieces):
                    wp = min(128, Wt - pi * 128)

                    def qt_r(t_, pi=pi, wp=wp, s2_c=s2_c, t=t):
                        p.wait(t_, "V", s2_c)
                        p.wait(t_, "V", qb_read[0])
                        if pi == 0:
                            p.wait(t_, "Sc", mm2ev_c[0])
                        p.inc("T", t_.transpose(QB[0:1, 256 + (pi % 2) * 128: 256 + (pi % 2) * 128 + wp],
                                                stats[0:wp, 20 + pi:21 + pi], eyef[0:wp, 0:wp]))
                    body("tensor", qt_r)
                    qtr = p.val("T")

                    def qv_rr(v, pi=pi, wp=wp, qtr=qtr):
                        p.wait(v, "T", qtr)
                        p.inc("V", v.tensor_copy(rowsb[0:1, pi * 128:pi * 128 + wp],
                                                 QB[0:1, 256 + (pi % 2) * 128:256 + (pi % 2) * 128 + wp]))
                    body("vector", qv_rr)
                    qrr_cnt[pi] = p.val("V")
                    qb_read[0] = qrr_cnt[pi]
                rr_c = p.val("V")

                def qt_bc(t_, Wt=Wt, rr_c=rr_c):
                    p.wait(t_, "V", rr_c)
                    p.inc("T", t_.matmul(C0[:, 0:Wt], ones1[:, :], rowsb[0:1, 0:Wt],
                                         start=True, stop=True))
                body("tensor", qt_bc)
                bc_c = p.val("T")
                if qstage < 5:
                    return

                def qv_rd(v, Wt=Wt, bc_c=bc_c, t=t):
                    p.wait(v, "T", bc_c)
                    if dump_hsil and t == NT - 1:
                        p.wait(v, "dF")
                    hs_c = hsil[:, :].rearrange("q (c w) -> q c w", w=260)[:, :, 0:Wt]
                    s2b = C0[:, 0:Wt].unsqueeze(1).broadcast_to([128, NHID, Wt])
                    p.inc("V", v.tensor_tensor(hs_c, hs_c, s2b, ALU.mult))
                body("vector", qv_rd)
                rd1 = p.val("V")
                if qstage < 6:
                    return

                def qg_rd(v, Wt=Wt, rd1=rd1):
                    hs_c = hsil[:, :].rearrange("q (c w) -> q c w", w=260)[:, :, 0:Wt]
                    v.tensor_scalar(hs_c, hs_c, smalls[:, 42:43], None, ALU.add)
                    p.inc("V", v.tensor_scalar(hs_c, hs_c, smalls[:, 42:43], None, ALU.subtract))
                body("vector", qg_rd)
                rd2 = p.val("V")
                if qstage < 7:
                    return

                def qv_rd2(v, Wt=Wt, rd2=rd2, t=t):
                    if t >= 1:
                        p.wait(v, "T", mm2_done[t - 1])
                    hs_c = hsil[:, :].rearrange("q (c w) -> q c w", w=260)[:, :, 0:Wt]
                    hq_c = hq[:, :].rearrange("q (c w) -> q c w", w=260)[:, :, 0:Wt]
                    p.inc("V", v.tensor_copy(hq_c, hs_c))
                body("vector", qv_rd2)
                hq_done[t] = p.val("V")

            mm2ev_c = [0]

            def emit_mm2(t):
                c_lo, c_hi, Wt, groups = tile_geom(t)
                gb = {}
                for gi in range(len(groups)):
                    gb[gi] = [C0, C1][gi] if gi < 2 else None

                piece_c = [0] * 9
                for j in range(8):
                    def w2_ld(sync, t=t, j=j):
                        if not (t == 0 and j < 2):
                            p.wait(sync, "T", piece_c[j - 2] if j >= 2 else mm2_prev_pc[0])
                            for c in range(4):
                                p.inc(f"dF{j % 2}", sync.dma_start(
                                    w2p[j % 2][:, c * D:(c + 1) * D],
                                    w2q_scr[(j * 4 + c) * 128:(j * 4 + c + 1) * 128, :]), 16)
                    body("sync", w2_ld)
                    ldF = p.val(f"dF{j % 2}")

                    def m2(t_, t=t, j=j, groups=groups, c_lo=c_lo, ldF=ldF):
                        p.wait(t_, f"dF{j % 2}", ldF)
                        if j == 0:
                            p.wait(t_, "V", hq_done[t])
                        for kk in range(4):
                            k = j * 4 + kk
                            for gi, (goff, gsz) in enumerate(groups):
                                lhs_src = hq[:, :].rearrange("q (c w) -> q c w", w=260)
                                lhs = lhs_src[:, k, goff:goff + gsz]
                                if gi < 2:
                                    bank = [C0, C1][gi]
                                    t_.matmul(bank[0:gsz, 0:512], lhs,
                                              w2p[j % 2][:, kk * D:kk * D + 512],
                                              start=(k == 0), stop=(k == DH // 128 - 1))
                                    inst = t_.matmul(bank[0:gsz, 512:1024], lhs,
                                                     w2p[j % 2][:, kk * D + 512:(kk + 1) * D],
                                                     start=(k == 0), stop=(k == DH // 128 - 1))
                                else:
                                    t_.matmul(Q0[0:gsz, 0:512], lhs,
                                              w2p[j % 2][:, kk * D:kk * D + 512],
                                              start=(k == 0), stop=(k == DH // 128 - 1))
                                    inst = t_.matmul(Q1[0:gsz, 0:512], lhs,
                                                     w2p[j % 2][:, kk * D + 512:(kk + 1) * D],
                                                     start=(k == 0), stop=(k == DH // 128 - 1))
                        p.inc("T", inst)
                    body("tensor", m2)
                    piece_c[j] = p.val("T")
                mm2_done[t] = p.val("T")
                mm2_prev_pc[0] = mm2_done[t]

                for gi, (goff, gsz) in enumerate(groups):
                    def r_ld(sync, t=t, gi=gi, goff=goff, gsz=gsz, c_lo=c_lo):
                        oi = out_idx[0] + gi
                        p.wait(sync, f"dO{oi % 2}")
                        p.inc(f"dR{oi % 2}", sync.dma_start(
                            x1s[oi % 2][0:gsz, :],
                            x1_scr[c_lo + goff: c_lo + goff + gsz, :]), 16)
                    body("sync", r_ld)
                    rld = p.val(f"dR{(out_idx[0] + gi) % 2}")

                    def r_ev(sc, t=t, gi=gi, gsz=gsz, rld=rld):
                        p.wait(sc, "T", mm2_done[t])
                        if dump_hsil and t == NT - 1 and gi == 1:
                            sc.copy(fdump[0:gsz, :].bitcast(F32) if False else ao2[0:gsz, :],
                                    C1[0:gsz, :])
                            p.inc("dF", sc.dma_start(dbg_p[0:gsz, :], ao2[0:gsz, :]), 16)
                        if gi < 2:
                            bank = [C0, C1][gi]
                            inst = sc.activation(ao2[0:gsz, :], bank[0:gsz, :], AF.Identity,
                                                 scale=stats[0:gsz, 48 + (t % 2) * 4 + gi:49 + (t % 2) * 4 + gi])
                        else:
                            sc.activation(ao2[0:gsz, 0:512], Q0[0:gsz, 0:512], AF.Identity,
                                          scale=stats[0:gsz, 48 + (t % 2) * 4 + gi:49 + (t % 2) * 4 + gi])
                            inst = sc.activation(ao2[0:gsz, 512:1024], Q1[0:gsz, 0:512],
                                                 AF.Identity, scale=stats[0:gsz, 48 + (t % 2) * 4 + gi:49 + (t % 2) * 4 + gi])
                        p.inc("Sc", sc.activation(fdump[0:gsz, :], ao2[0:gsz, :], AF.Square,
                                                  accum_out=stats[0:gsz, 28:29]))
                    body("scalar", r_ev)
                    rev = p.val("Sc")

                    def r_v1(v, gsz=gsz, rev=rev):
                        p.wait(v, "Sc", rev)
                        p.flush(v, "V", v.reduce_sum(stats[0:gsz, 29:30].unsqueeze(2),
                                     ao2[0:gsz, :].unsqueeze(1), AX.X))
                        v.tensor_scalar(stats[0:gsz, 30:31], stats[0:gsz, 29:30], 1.0 / D,
                                        None, ALU.mult)
                        p.flush(v, "V", v.tensor_scalar(stats[0:gsz, 31:32], stats[0:gsz, 28:29],
                                        1.0 / D, EPS_LN, ALU.mult, ALU.add))
                        p.flush(v, "V", v.tensor_mul(stats[0:gsz, 32:33], stats[0:gsz, 30:31],
                                        stats[0:gsz, 30:31]))
                        p.flush(v, "V", v.tensor_sub(stats[0:gsz, 31:32], stats[0:gsz, 31:32],
                                        stats[0:gsz, 32:33]))
                        p.inc("V", v.reciprocal(stats[0:gsz, 31:32], stats[0:gsz, 31:32]))
                    body("vector", r_v1)
                    rv1 = p.val("V")

                    def r_s2(sc, gsz=gsz, rv1=rv1):
                        p.wait(sc, "V", rv1)
                        p.inc("Sc", sc.activation(stats[0:gsz, 31:32], stats[0:gsz, 31:32],
                                                  AF.Sqrt))
                    body("scalar", r_s2)
                    rs2 = p.val("Sc")

                    def r_v2(v, gsz=gsz, rs2=rs2):
                        p.wait(v, "Sc", rs2)
                        p.wait(v, "V")
                        p.inc("V", v.scalar_tensor_tensor(stats[0:gsz, 33:34],
                                        stats[0:gsz, 30:31], -1.0, stats[0:gsz, 31:32],
                                        ALU.mult, ALU.mult))
                    body("vector", r_v2)
                    rv2 = p.val("V")

                    def r_s3(sc, gi=gi, gsz=gsz, rv2=rv2, rld=rld):
                        p.wait(sc, "V", rv2)
                        oi_ = out_idx[0] + gi
                        p.wait(sc, f"dR{oi_ % 2}", rld)
                        oi = out_idx[0] + gi
                        p.inc("Sc", sc.activation(x1s[oi % 2][0:gsz, :], x1s[oi % 2][0:gsz, :],
                                                  AF.Identity, bias=stats[0:gsz, 33:34]))
                    body("scalar", r_s3)
                    rs3 = p.val("Sc")

                    def r_v3(v, gi=gi, gsz=gsz, rs3=rs3):
                        p.wait(v, "Sc", rs3)
                        oi = out_idx[0] + gi
                        p.inc("V", v.scalar_tensor_tensor(outb[oi % 2][0:gsz, :], ao2[0:gsz, :],
                                                          stats[0:gsz, 31:32],
                                                          x1s[oi % 2][0:gsz, :],
                                                          ALU.mult, ALU.add))
                    body("vector", r_v3)
                    rv3 = p.val("V")

                    def r_st(sc, t=t, gi=gi, goff=goff, gsz=gsz, c_lo=c_lo, rv3=rv3):
                        p.wait(sc, "V", rv3)
                        oi = out_idx[0] + gi
                        p.inc(f"dO{oi % 2}", sc.dma_start(
                            out[c_lo + goff: c_lo + goff + gsz, :], outb[oi % 2][0:gsz, :]), 16)
                    body("scalar", r_st)
                out_idx[0] += len(groups)
                mm2ev_c[0] = p.val("Sc")

            mm2_prev_pc = [0]


            fs = [0]

            def go():
                fs[0] += 1
                return fs[0] <= fstage

            if go():
                emit_ln2(0)
                emit_ln2_tail(0)
            if go():
                emit_mm1(0)
            if go():
                emit_ln2(1)
                emit_ln2_tail(1)
            if go():
                emit_quant2(0)
            for t in range(1, NT):
                if t + 1 < NT:
                    if go():
                        emit_ln2(t + 1)
                if go():
                    emit_mm1(t)
                if go():
                    emit_mm2(t - 1)
                if t + 1 < NT:
                    if go():
                        emit_ln2_tail(t + 1)
                if go():
                    emit_quant2(t)
            if go():
                emit_mm2(NT - 1)

            def fin(sc):
                p.wait(sc, "dO0")
                p.wait(sc, "dO1")
            body("scalar", fin)


    return nc


def make_inputs(core_x, ins):
    d = {}
    d["x"] = np.ascontiguousarray(core_x.astype(np.float32, copy=False))
    for m, key in [("r", "L0_attn_w_r"), ("k", "L0_attn_w_k"), ("v", "L0_attn_w_v"),
                   ("g", "L0_attn_w_g"), ("o", "L0_attn_w_o")]:
        d[f"w{m}T"] = np.ascontiguousarray(np.asarray(ins[key]).T)
    d["w1T"] = np.ascontiguousarray(np.asarray(ins["L0_ffn_w1"]).T)
    d["w2T"] = np.ascontiguousarray(np.asarray(ins["L0_ffn_w2"]).T)
    maa = np.stack([np.asarray(ins[f"L0_attn_time_maa_{m}"]) for m in "rkvg"], 0)
    d["maa"] = np.ascontiguousarray(maa.reshape(4, 8, 128).transpose(2, 0, 1).reshape(128, 32))
    decay = np.asarray(ins["L0_attn_time_decay"])
    d["lnwbc"] = np.ascontiguousarray(np.broadcast_to(np.repeat(decay, HN)[None, :], (128, D)))
    d["decay_b"] = np.ascontiguousarray(np.broadcast_to(decay[None, :], (128, H)))
    d["faaaa_b"] = np.ascontiguousarray(
        np.broadcast_to(np.asarray(ins["L0_attn_time_faaaa"])[None, :], (128, H)))
    d["decay_s"] = np.ascontiguousarray(np.broadcast_to(decay[None, :], (128, H)).astype(np.float32))
    d["lng_bc"] = np.ascontiguousarray(
        np.broadcast_to(np.asarray(ins["L0_attn_ln_g"])[None, :], (128, D)))
    d["lnb_bc"] = np.ascontiguousarray(
        np.broadcast_to(np.asarray(ins["L0_attn_ln_b"])[None, :], (128, D)))
    cw = np.asarray(ins["L0_ffn_conv_w"]).reshape(DH, 3)
    d["convw"] = np.ascontiguousarray(
        cw.reshape(NHID, 128, 3).transpose(1, 2, 0).reshape(128, 3 * NHID))
    d["convb"] = np.ascontiguousarray(np.asarray(ins["L0_ffn_conv_b"]).reshape(NHID, 128).T)
    return d


def kernel(**inputs):
    x = np.asarray(inputs["L0_attn_x"])
    B, T, _ = x.shape
    nc = build_program(T=T, stub_ffn=False)
    in_maps = [make_inputs(x[b], inputs) for b in range(B)]
    res = run_bass_kernel_spmd(nc, in_maps, core_ids=list(range(B)))
    return np.stack([res.results[b]["out"] for b in range(B)], 0)
